# version 13
# baseline (speedup 1.0000x reference)
import os
import math
import contextlib

os.environ.setdefault("TRN_ALLOW_ICI_ALL_TO_ALL", "1")

import numpy as np
import ml_dtypes

import concourse.bass as bass
import concourse.mybir as mybir
from concourse.bass_utils import run_bass_kernel_spmd
from concourse.aot_env import allow_ici_all_to_all

F32 = mybir.dt.float32
BF16 = mybir.dt.bfloat16
AF = mybir.ActivationFunctionType
ALU = mybir.AluOpType
BF = ml_dtypes.bfloat16

NCORE = 8
S = 8192
D = 4096
HD = 128
HPC = 4
FPC = HPC * HD
TPC = S // NCORE
SLAB = 512
NSLAB = S // SLAB
QP = 512
NPAN = S // QP
SCL = 1.0 / math.sqrt(HD)
CEXP = -10.0
EPS = 1e-5
NXB = 4

ENGINES = ("sync", "tensor", "vector", "scalar", "gpsimd")
DMA_SEMS = (["qC", "qCN", "qON", "qID", "qE8", "qM", "qB", "qWQ", "qWK", "qWV", "qCSa", "qCSb", "qKV", "qKVb", "qQa", "qQb", "qOG", "qOGB", "qXS", "qY", "qPCa", "qPCb"]
            + [f"qX{i}" for i in range(NXB)]
            + ["qR0", "qR1"] + [f"qSV{i}" for i in range(4)]
            + ["qO0", "qO1"] + [f"qW{i}" for i in range(8)])
ALL_SEMS = DMA_SEMS + ["sPE", "sDVE", "sACT", "sPOOL", "sCCa", "sCCb", "sBAR"]


def build_program():
    nc = bass.Bass("TRN2", target_bir_lowering=False, debug=False)

    XT = nc.dram_tensor("xt", [D, S], F32, kind="ExternalInput").ap()
    XR = nc.dram_tensor("xr", [TPC, D], F32, kind="ExternalInput").ap()
    WQ = nc.dram_tensor("wq", [128, 32 * FPC], F32, kind="ExternalInput").ap()
    WK = nc.dram_tensor("wk", [128, 32 * FPC], F32, kind="ExternalInput").ap()
    WV = nc.dram_tensor("wv", [128, 32 * FPC], F32, kind="ExternalInput").ap()
    WO = nc.dram_tensor("wo", [D, D], F32, kind="ExternalInput").ap()
    WT = nc.dram_tensor("wt", [D, D], F32, kind="ExternalInput").ap()
    BREP = nc.dram_tensor("brep", [128, D], F32, kind="ExternalInput").ap()
    ATM = nc.dram_tensor("atm", [128, 32 * 128], F32, kind="ExternalInput").ap()
    BMT = nc.dram_tensor("bmt", [128, D], F32, kind="ExternalInput").ap()
    WRT = nc.dram_tensor("wrt", [128, 128], F32, kind="ExternalInput").ap()
    CS2 = nc.dram_tensor("cs2", [128, 2 * S], F32, kind="ExternalInput").ap()
    MSK = nc.dram_tensor("msk", [128, 4 * QP], BF16, kind="ExternalInput").ap()
    ONESB = nc.dram_tensor("onesb", [128, 128], BF16, kind="ExternalInput").ap()
    IDENT = nc.dram_tensor("ident", [128, 128], BF16, kind="ExternalInput").ap()
    E8P = nc.dram_tensor("e8p", [4, 128], BF16, kind="ExternalInput").ap()
    CONSTS = nc.dram_tensor("consts", [128, 16], F32, kind="ExternalInput").ap()
    Y = nc.dram_tensor("y", [TPC, D], F32, kind="ExternalOutput").ap()

    QTd = nc.dram_tensor("qtd", [FPC, S], BF16)
    KTd = nc.dram_tensor("ktd", [FPC, S], BF16)
    Vd = nc.dram_tensor("vd", [S, FPC], BF16)
    CCIA = nc.dram_tensor("ccia", [D // 2, TPC], BF16)
    CCIB = nc.dram_tensor("ccib", [D // 2, TPC], BF16)
    CCOA = nc.dram_tensor("ccoa", [D // 2, TPC], BF16)
    CCOB = nc.dram_tensor("ccob", [D // 2, TPC], BF16)
    WOB = nc.dram_tensor("wob", [8 * 32 * 128, 512], BF16)
    WTB = nc.dram_tensor("wtb", [8 * 32 * 128, 512], BF16)

    OPS = {e: [] for e in ENGINES}
    n = {"PE": 0, "DVE": 0, "ACT": 0, "POOL": 0, "BAR": 0}
    for s_ in DMA_SEMS:
        n[s_] = 0
    hwm = {e: {} for e in ENGINES}

    def _w(e, sem, val):
        if val is None or val <= 0:
            return
        if hwm[e].get(sem, 0) >= val:
            return
        hwm[e][sem] = val
        OPS[e].append(("w", sem, val))

    def _i(e, fn, sem=None, inc=1):
        OPS[e].append(("i", fn, sem, inc))

    def load(fn, sem):
        n[sem] += 16
        _i("sync", fn, sem, 16)
        return (sem, n[sem])

    def gload(fn, sem):
        n[sem] += 16
        _i("gpsimd", fn, sem, 16)
        return (sem, n[sem])

    def loadq(engq, fn, sem):
        n[sem] += 16
        _i(engq, fn, sem, 16)
        return (sem, n[sem])

    def store(fn, sem):
        n[sem] += 16
        _i("scalar", fn, sem, 16)
        return (sem, n[sem])

    def pe(fn, inc=False):
        if inc:
            n["PE"] += 1
            _i("tensor", fn, "sPE", 1)
            return n["PE"]
        _i("tensor", fn)
        return None

    def dve(fn, inc=True):
        _w("vector", "sDVE", n["DVE"])
        n["DVE"] += 1
        _i("vector", fn, "sDVE", 1)
        return n["DVE"]

    def act(fn, inc=True):
        _w("scalar", "sACT", n["ACT"])
        n["ACT"] += 1
        _i("scalar", fn, "sACT", 1)
        return n["ACT"]

    def pool(fn, inc=True):
        _w("gpsimd", "sPOOL", n["POOL"])
        n["POOL"] += 1
        _i("gpsimd", fn, "sPOOL", 1)
        return n["POOL"]

    def barrier():
        n["BAR"] += 1
        tgt = 5 * n["BAR"]
        for e in ENGINES:
            _i(e, lambda eng: eng.nop(), "sBAR", 1)
        for e in ENGINES:
            hwm[e]["sBAR"] = tgt
            OPS[e].append(("w", "sBAR", tgt))

    def wait_all_stores(e):
        for s_ in ("qR0", "qR1", "qSV0", "qSV1", "qSV2", "qSV3", "qO0", "qO1", "qY"):
            _w(e, s_, n[s_])

    es = contextlib.ExitStack()

    sconst = es.enter_context(nc.sbuf_tensor("sconst", [128, 16], F32))
    sOnes = es.enter_context(nc.sbuf_tensor("sOnes", [128, 128], BF16))
    sIdent = es.enter_context(nc.sbuf_tensor("sIdent", [128, 128], BF16))
    sE8 = es.enter_context(nc.sbuf_tensor("sE8", [4, 128], BF16))
    sAtb = es.enter_context(nc.sbuf_tensor("sAtb", [128, 32 * 128], BF16))
    sWrTb = es.enter_context(nc.sbuf_tensor("sWrTb", [128, 128], BF16))
    sBm = es.enter_context(nc.sbuf_tensor("sBm", [128, D], BF16))
    sBrep = es.enter_context(nc.sbuf_tensor("sBrep", [128, D], F32))
    sWb = [es.enter_context(nc.sbuf_tensor(f"sWb{i}", [128, 512], BF16)) for i in range(8)]
    sOg = es.enter_context(nc.sbuf_tensor("sOg", [128, 32 * 512], BF16))

    stripe_user = {}
    kglob = [0]

    def stripe(which, ic, oslab):
        k = kglob[0]
        kb = k % 8
        eng = "sync" if kb % 2 == 0 else "gpsimd"
        Wb, pcs = (WOB, "qPCa") if which == "o" else (WTB, "qPCb")
        _w(eng, "sPE", stripe_user.get(k - 8, 0))
        _w(eng, pcs, 32 * 16)
        r0 = (oslab * 32 + ic) * 128
        lw = loadq(eng, lambda e, r0=r0, kb=kb, Wb=Wb: e.dma_start(
            out=sWb[kb][:], in_=Wb.ap()[r0:r0 + 128, :]), f"qW{kb}")
        kglob[0] += 1
        return k, kb, lw

    CONST_VALS = [CEXP, SCL, -1.0, -0.5, EPS, 1.0, 1.0 / D, -1.0 / D, 0.0, 2.0, 0.5, -2.0]
    for j, v in enumerate(CONST_VALS):
        nc.const_aps.register(F32, float(v), sconst[:, j:j + 1])

    with contextlib.ExitStack() as p1:
        swq = p1.enter_context(nc.sbuf_tensor("swq", [128, 32 * FPC], BF16))
        swk = p1.enter_context(nc.sbuf_tensor("swk", [128, 32 * FPC], BF16))
        sxb = [p1.enter_context(nc.sbuf_tensor(f"sxb{i}", [128, 4 * SLAB], BF16)) for i in range(NXB)]
        scs = [p1.enter_context(nc.sbuf_tensor(f"scs{i}", [128, 2 * SLAB], F32)) for i in range(2)]
        sQf = [p1.enter_context(nc.sbuf_tensor(f"sQf{i}", [128, SLAB], F32)) for i in range(8)]
        sRot = [p1.enter_context(nc.sbuf_tensor(f"sRot{i}", [128, SLAB], F32)) for i in range(4)]
        sQc = [p1.enter_context(nc.sbuf_tensor(f"sQc{i}", [128, SLAB], F32)) for i in range(2)]
        sQs = [p1.enter_context(nc.sbuf_tensor(f"sQs{i}", [128, SLAB], F32)) for i in range(2)]
        sQr = [p1.enter_context(nc.sbuf_tensor(f"sQr{i}", [128, SLAB], BF16)) for i in range(2)]
        pQK = [p1.enter_context(nc.psum_tensor(f"pQK{j}", [128, SLAB], F32)) for j in range(8)]

        load(lambda e: e.dma_start(out=sconst[:], in_=CONSTS), "qCN")
        load(lambda e: e.dma_start(out=sOnes[:], in_=ONESB), "qON")
        load(lambda e: e.dma_start(out=sIdent[:], in_=IDENT), "qID")
        load(lambda e: e.dma_start(out=sE8[:], in_=E8P), "qE8")
        load(lambda e: e.dma_start(out=sBrep[:], in_=BREP), "qB")
        gload(lambda e: e.dma_start(out=sAtb[:], in_=ATM), "qC")
        gload(lambda e: e.dma_start(out=sWrTb[:], in_=WRT), "qC")
        gload(lambda e: e.dma_start(out=sBm[:], in_=BMT), "qC")
        p4c_total = n["qC"]
        wq_l = gload(lambda e: e.dma_start(out=swq[:], in_=WQ), "qWQ")
        wk_l = gload(lambda e: e.dma_start(out=swk[:], in_=WK), "qWK")

        xload = {}
        xconv = {}
        icgrp = {}
        bankfree = {}
        rotdone = {}
        st_qk = {}
        cs_load = {}

        def p1a_finish_slab(ts):
            cb = ts % 2
            for j in range(8):
                _w("scalar", "sPE", icgrp[ts * 8 + 7])
                _w("scalar", "sDVE", rotdone.get((ts - 1, j), 0))
                bankfree[(ts, j)] = act(lambda e, j=j: e.copy(sQf[j][:], pQK[j][:]))
                _w("vector", "sACT", bankfree[(ts, j)])
                pj4 = (ts, j - 4) if j >= 4 else (ts - 1, j + 4)
                _w("vector", "sDVE", rotdone.get(pj4, 0))
                dve(lambda e, j=j, b=j % 4: e.tensor_scalar_mul(sRot[b][0:64, :], sQf[j][64:128, :], -1.0))
                rp = dve(lambda e, j=j, b=j % 4: e.tensor_copy(sRot[b][64:128, :], sQf[j][0:64, :]))
                _w("vector", *cs_load[ts])
                pj = (ts, j - 2) if j >= 2 else (ts - 1, j + 6)
                if pj in st_qk:
                    _w("vector", *st_qk[pj])
                dve(lambda e, j=j, b=j % 2, cb=cb: e.tensor_tensor(sQc[b][:], sQf[j][:], scs[cb][:, 0:SLAB], ALU.mult))
                dve(lambda e, b=j % 4, bb=j % 2, cb=cb: e.tensor_tensor(sQs[bb][:], sRot[b][:], scs[cb][:, SLAB:2 * SLAB], ALU.mult))
                rotdone[(ts, j)] = dve(lambda e, b=j % 2: e.tensor_tensor(sQr[b][:], sQc[b][:], sQs[b][:], ALU.add))
                _w("scalar", "sDVE", rotdone[(ts, j)])
                dst = QTd if j < 4 else KTd
                oc = j % 4
                st_qk[(ts, j)] = store(lambda e, dst=dst, oc=oc, tp=SLAB * ts, b=j % 2: e.dma_start(
                    out=dst.ap()[128 * oc:128 * (oc + 1), tp:tp + SLAB], in_=sQr[b][:]),
                    "qR0" if j % 2 == 0 else "qR1")

        for ts in range(NSLAB):
            t0 = SLAB * ts
            _w("sync", "sDVE", rotdone.get((ts - 2, 7), 0))
            cs_load[ts] = load(lambda e, ts=ts, b=ts % 2: e.dma_start(
                out=scs[b][:], in_=CS2[:, 1024 * ts:1024 * (ts + 1)]),
                "qCSa" if ts % 2 == 0 else "qCSb")
            def p1a_loadconv(gq):
                g = ts * 8 + gq
                _w("gpsimd", "sPE", icgrp.get(g - NXB, 0))
                xload[g] = gload(lambda e, t0=t0, gq=gq, b=g % NXB: e.dma_start(
                    out=sxb[b][:].rearrange("p (c t) -> p c t", c=4),
                    in_=XT.rearrange("(c p) t -> p c t", p=128)[:, 4 * gq:4 * (gq + 1), t0:t0 + SLAB]),
                    f"qX{g % NXB}")
                xconv[g] = xload[g]

            for gq in range(2):
                p1a_loadconv(gq)
            if ts >= 1:
                p1a_finish_slab(ts - 1)
            if ts == 0:
                _w("tensor", *wq_l)
                _w("tensor", *wk_l)
            for ic in range(32):
                g = ts * 8 + ic // 4
                if ic >= 8 and ic % 4 == 0:
                    p1a_loadconv(ic // 4)
                _w("tensor", *xconv[g])
                for j in range(8):
                    if ic == 0:
                        _w("tensor", "sACT", bankfree.get((ts - 1, j), 0))
                    src = swq if j < 4 else swk
                    oc = j % 4
                    val = pe(lambda e, j=j, ic=ic, src=src, oc=oc, b=g % NXB, st=(ic == 0), sp=(ic == 31):
                             e.matmul(pQK[j][:], src[:, ic * FPC + oc * 128: ic * FPC + (oc + 1) * 128],
                                      sxb[b][:, (ic % 4) * SLAB:(ic % 4 + 1) * SLAB], start=st, stop=sp),
                             inc=(j == 7))
                    if j == 7:
                        icgrp[ts * 8 + ic // 4] = val

        p1a_finish_slab(NSLAB - 1)

    barrier()

    with contextlib.ExitStack() as p1b:
        swv = p1b.enter_context(nc.sbuf_tensor("swv", [128, 32 * FPC], BF16))
        sxbB = [p1b.enter_context(nc.sbuf_tensor(f"sxbB{i}", [128, 4 * SLAB], BF16)) for i in range(NXB)]
        sVe = [p1b.enter_context(nc.sbuf_tensor(f"sVe{i}", [128, FPC], BF16)) for i in range(4)]
        pV = [p1b.enter_context(nc.psum_tensor(f"pV{j}", [128, FPC], F32)) for j in range(8)]

        wv_l = gload(lambda e: e.dma_start(out=swv[:], in_=WV), "qWV")

        xloadB, xconvB, icgrpB, bfreeB, stV = {}, {}, {}, {}, {}

        def p1b_evacs(ts):
            half = (ts % 2) * 4
            for tt in range(4):
                _w("scalar", "sPE", icgrpB[ts * 8 + 7])
                if (ts - 1, tt) in stV:
                    _w("scalar", *stV[(ts - 1, tt)])
                bfreeB[(ts, tt)] = act(lambda e, tt=tt, half=half: e.copy(sVe[tt][:], pV[half + tt][:]))
                _w("scalar", "sACT", bfreeB[(ts, tt)])
                stV[(ts, tt)] = store(lambda e, tp=SLAB * ts, tt=tt: e.dma_start(
                    out=Vd.ap()[tp + 128 * tt: tp + 128 * (tt + 1), :], in_=sVe[tt][:]), f"qSV{tt}")

        for ts in range(NSLAB):
            t0 = SLAB * ts
            def p1b_loadconv(gq):
                g = ts * 8 + gq
                _w("gpsimd", "sPE", icgrpB.get(g - NXB, 0))
                xloadB[g] = gload(lambda e, t0=t0, gq=gq, b=g % NXB: e.dma_start(
                    out=sxbB[b][:].rearrange("p (c t) -> p c t", c=4),
                    in_=XT.rearrange("(c p) t -> p c t", p=128)[:, 4 * gq:4 * (gq + 1), t0:t0 + SLAB]),
                    f"qX{g % NXB}")
                xconvB[g] = xloadB[g]

            for gq in range(2):
                p1b_loadconv(gq)
            if ts >= 1:
                p1b_evacs(ts - 1)
            if ts == 0:
                _w("tensor", *wv_l)
            half = (ts % 2) * 4
            for ic in range(32):
                g = ts * 8 + ic // 4
                if ic >= 8 and ic % 4 == 0:
                    p1b_loadconv(ic // 4)
                _w("tensor", *xconvB[g])
                for tt in range(4):
                    if ic == 0:
                        _w("tensor", "sACT", bfreeB.get((ts - 2, tt), 0))
                    val = pe(lambda e, tt=tt, ic=ic, b=g % NXB, st=(ic == 0), sp=(ic == 31), half=half:
                             e.matmul(pV[half + tt][:], sxbB[b][:, (ic % 4) * SLAB + tt * 128:(ic % 4) * SLAB + (tt + 1) * 128],
                                      swv[:, ic * FPC:(ic + 1) * FPC], start=st, stop=sp),
                             inc=(tt == 3))
                    if tt == 3:
                        icgrpB[g] = val
        p1b_evacs(NSLAB - 1)

    barrier()

    with contextlib.ExitStack() as p2:
        sMask = p2.enter_context(nc.sbuf_tensor("sMask", [128, 4 * QP], BF16))
        skT = [p2.enter_context(nc.sbuf_tensor(f"skT{i}", [128, S], BF16)) for i in range(2)]
        sv = [p2.enter_context(nc.sbuf_tensor(f"sv{i}", [128, S], BF16)) for i in range(2)]
        sqTp = [p2.enter_context(nc.sbuf_tensor(f"sqTp{i}", [128, QP], BF16)) for i in range(2)]
        sP = [p2.enter_context(nc.sbuf_tensor(f"sP{i}", [128, QP], BF16)) for i in range(4)]
        sPt = [p2.enter_context(nc.sbuf_tensor(f"sPt{i}", [128, QP], BF16)) for i in range(2)]
        sRec = [p2.enter_context(nc.sbuf_tensor(f"sRec{i}", [128, QP], F32)) for i in range(2)]
        sLn2b = [p2.enter_context(nc.sbuf_tensor(f"sLn2b{i}", [128, QP], F32)) for i in range(2)]
        sOn = [p2.enter_context(nc.sbuf_tensor(f"sOn{i}", [128, QP], BF16)) for i in range(2)]
        pS = [p2.enter_context(nc.psum_tensor(f"pS{i}", [128, QP], F32)) for i in range(3)]
        pO = [p2.enter_context(nc.psum_tensor(f"pO{i}", [128, QP], F32)) for i in range(2)]
        pSm = [p2.enter_context(nc.psum_tensor(f"pSm{i}", [128, QP], F32)) for i in range(2)]

        for src, dst, pcs in ((WO, WOB, "qPCa"), (WT, WTB, "qPCb")):
            for ch in range(32):
                gload(lambda e, src=src, dst=dst, ch=ch: e.dma_start(
                    out=dst.ap().rearrange("(o r) c -> o r c", o=8)[:, ch * 128:(ch + 1) * 128, :],
                    in_=src[128 * ch:128 * (ch + 1), :].rearrange("p (o c) -> o p c", o=8)), pcs)

        for s_ in ("qR0", "qR1", "qSV0", "qSV1", "qSV2", "qSV3"):
            _w("sync", s_, n[s_])
        lmask = load(lambda e: e.dma_start(out=sMask[:], in_=MSK), "qM")
        _w("scalar", "qCN", 16)
        _w("vector", *lmask)
        _w("tensor", "qON", 16)

        qload = {}
        kv_done = {}
        qbuf_user = {}
        pS_user = {}
        sP_user = {}
        sPt_user = {}
        pO_user = {}
        pSm_user = {}
        sOn_user = {}
        head_pe_end = 0

        head_ends = {}

        def load_head_kv(h):
            hb = h % 2
            sem = "qKV" if hb == 0 else "qKVb"
            load(lambda e, h=h, hb=hb: e.dma_start(out=skT[hb][:], in_=KTd.ap()[128 * h:128 * (h + 1), :]), sem)
            for g in range(8):
                load(lambda e, h=h, g=g, hb=hb: e.dma_start(
                    out=sv[hb][:, g * 1024:(g + 1) * 1024].rearrange("p (kt d) -> p kt d", d=128),
                    in_=Vd.ap().rearrange("(kt p) (hh d) -> p kt hh d", p=128, d=128)[:, 8 * g:8 * (g + 1), h, :]), sem)
            kv_done[h] = (sem, n[sem])

        load_head_kv(0)
        qo_after_h1 = None
        for h in range(HPC):
            hb = h % 2
            if h + 1 < HPC:
                _w("sync", "sPE", head_ends.get(h - 1, 0))
                load_head_kv(h + 1)

            for p in range(NPAN):
                nkt = 4 * (p + 1)
                qb = p % 2
                ob = p % 2
                _w("sync", "sPE", qbuf_user.get(qb, 0))
                qload[(h, p)] = load(lambda e, h=h, p=p, qb=qb: e.dma_start(
                    out=sqTp[qb][:], in_=QTd.ap()[128 * h:128 * (h + 1), QP * p:QP * (p + 1)]),
                    "qQa" if qb == 0 else "qQb")

                exp_done = {}
                last_sum = 0

                def emit_score(kt):
                    sb = kt % 3
                    if kt == 0:
                        _w("tensor", *qload[(h, p)])
                        _w("tensor", *kv_done[h])
                    _w("tensor", "sACT", pS_user.get(sb, 0))
                    v = pe(lambda e, kt=kt, sb=sb, qb=qb, hb=hb: e.matmul(
                        pS[sb][:], skT[hb][:, kt * 128:(kt + 1) * 128], sqTp[qb][:], start=True, stop=True), inc=True)
                    qbuf_user[qb] = v
                    return v

                peS = {0: emit_score(0)}
                if nkt > 1:
                    peS[1] = emit_score(1)
                for kt in range(nkt):
                    sb = kt % 3
                    pb = kt % 4
                    diag = (kt >= 4 * p)
                    _w("scalar", "sPE", peS[kt])
                    if diag:
                        _w("scalar", "sDVE", sPt_user.get(kt % 2, 0))
                        tgt = sPt[kt % 2]
                    else:
                        _w("scalar", "sPE", sP_user.get(pb, 0))
                        tgt = sP[pb]
                    exp_done[kt] = act(lambda e, sb=sb, tgt=tgt: e.activation(
                        tgt[:], pS[sb][:], AF.Exp, bias=CEXP, scale=SCL), inc=True)
                    pS_user[sb] = exp_done[kt]
                    if diag:
                        dv = kt - 4 * p
                        _w("vector", "sACT", exp_done[kt])
                        _w("vector", "sPE", sP_user.get(pb, 0))
                        mk_ = dve(lambda e, pb=pb, tb=kt % 2, dv=dv: e.tensor_tensor(
                            sP[pb][:], sPt[tb][:], sMask[:, dv * QP:(dv + 1) * QP], ALU.mult))
                        sPt_user[kt % 2] = mk_
                    if kt + 2 < nkt:
                        peS[kt + 2] = emit_score(kt + 2)
                    if diag:
                        _w("tensor", "sDVE", mk_)
                    else:
                        _w("tensor", "sACT", exp_done[kt])
                    if kt == 0:
                        _w("tensor", "sDVE", pO_user.get(ob, 0))
                        _w("tensor", "sACT", pSm_user.get(ob, 0))
                    pe(lambda e, kt=kt, pb=pb, ob=ob, sp=(kt == nkt - 1), hb=hb: e.matmul(
                        pO[ob][:], sv[hb][:, kt * 128:(kt + 1) * 128], sP[pb][:],
                        start=(kt == 0), stop=sp))
                    last_sum = pe(lambda e, kt=kt, pb=pb, ob=ob, sp=(kt == nkt - 1): e.matmul(
                        pSm[ob][:], sOnes[:], sP[pb][:],
                        start=(kt == 0), stop=sp), inc=True)
                    sP_user[pb] = last_sum
                head_pe_end = last_sum
                head_ends[h] = last_sum
                _w("scalar", "sPE", last_sum)
                _w("scalar", "sACT", pSm_user.get(ob, 0))
                lnc = act(lambda e, ob=ob: e.activation(sLn2b[ob][:], pSm[ob][:], AF.Ln), inc=True)
                _w("scalar", "sACT", lnc)
                epi = act(lambda e, ob=ob: e.activation(sRec[ob][:], sLn2b[ob][:], AF.Exp, scale=-1.0), inc=True)
                pSm_user[ob] = epi
                _w("vector", "sACT", epi)
                if ob in sOn_user:
                    _w("vector", *sOn_user[ob])
                nrm = dve(lambda e, ob=ob: e.tensor_tensor(sOn[ob][:], pO[ob][:], sRec[ob][:], ALU.mult), inc=True)
                pO_user[ob] = nrm
                _w("scalar", "sDVE", nrm)
                cct = CCIA if h < 2 else CCIB
                r0 = 256 * (p // 2) + 128 * (h % 2)
                c0 = 512 * (p % 2)
                sOn_user[ob] = store(lambda e, cct=cct, r0=r0, c0=c0, ob=ob: e.dma_start(
                    out=cct.ap()[r0:r0 + 128, c0:c0 + QP], in_=sOn[ob][:]), "qO0" if ob == 0 else "qO1")
            if h == 1:
                qo_after_h1 = (n["qO0"], n["qO1"])

    _w("gpsimd", "qO0", qo_after_h1[0])
    _w("gpsimd", "qO1", qo_after_h1[1])
    with allow_ici_all_to_all():
        _i("gpsimd", lambda e: e.collective_compute(
            "AllToAll", ALU.bypass, replica_groups=[list(range(NCORE))],
            ins=[CCIA.ap().opt()], outs=[CCOA.ap().opt()]), "sCCa", 1)
    wait_all_stores("gpsimd")
    with allow_ici_all_to_all():
        _i("gpsimd", lambda e: e.collective_compute(
            "AllToAll", ALU.bypass, replica_groups=[list(range(NCORE))],
            ins=[CCIB.ap().opt()], outs=[CCOB.ap().opt()]), "sCCb", 1)

    _w("sync", "sCCa", 1)
    for hh in range(2):
        loadq("sync", lambda e, hh=hh: e.dma_start(
            out=sOg[:].rearrange("p (r q) -> p r q", q=2048)[:, :, 512 * hh:512 * (hh + 1)],
            in_=CCOA.ap().rearrange("(r hh p) t -> p r hh t", p=128, hh=2)[:, :, hh, 0:512]), "qOG")
    ogA_pre = n["qOG"]
    IC_ORDER = [ic for ic in range(32) if ic % 4 < 2] + [ic for ic in range(32) if ic % 4 >= 2]
    pre8 = [stripe("o", IC_ORDER[i], 0) for i in range(8)]

    barrier()

    with contextlib.ExitStack() as p4:
        sAct = p4.enter_context(nc.sbuf_tensor("sAct", [128, 4 * D], F32))
        sH1bt = p4.enter_context(nc.sbuf_tensor("sH1bt", [128, D], BF16))
        sH1T = p4.enter_context(nc.sbuf_tensor("sH1T", [128, 32 * 512], BF16))
        sXs = [p4.enter_context(nc.sbuf_tensor(f"sXs{i}", [128, 4 * 512], F32)) for i in range(1)]
        sStat = p4.enter_context(nc.sbuf_tensor("sStat", [128, 8], F32))
        sGeb = p4.enter_context(nc.sbuf_tensor("sGeb", [4, 512], BF16))
        sGl = p4.enter_context(nc.sbuf_tensor("sGl", [4, 512], BF16))
        sR4 = p4.enter_context(nc.sbuf_tensor("sR4", [4, 512], BF16))
        sGn = p4.enter_context(nc.sbuf_tensor("sGn", [4, 512], BF16))
        sPr32 = p4.enter_context(nc.sbuf_tensor("sPr32", [128, 512], F32))
        sPrg = p4.enter_context(nc.sbuf_tensor("sPrg", [128, 512], BF16))
        pAo = [p4.enter_context(nc.psum_tensor(f"pAo{i}", [128, 512], F32)) for i in range(4)]
        pTr = [p4.enter_context(nc.psum_tensor(f"pTr{i}", [128, 1024], BF16)) for i in range(2)]
        pSmall = p4.enter_context(nc.psum_tensor("pSmall", [128, 512], F32))
        pSmall2 = p4.enter_context(nc.psum_tensor("pSmall2", [128, 512], F32))

        pAo_user = {tt: 0 for tt in range(4)}
        sXs_user = 0
        pTr_user = {0: 0, 1: 0}
        y_store_total = 0
        outproj_pe_end = 0
        prg_prev = 0
        pSm2_user = 0

        def emit_ln(tt, ready_dve, ready_pe=0):
            _w("scalar", "sDVE", ready_dve)
            _w("scalar", "sPE", ready_pe)
            act(lambda e, tt=tt: e.activation(sH1bt[:], sAct[:, tt * D:(tt + 1) * D],
                                              AF.Square, accum_out=sStat[:, 1:2]))
            sm = act(lambda e, tt=tt: e.activation(sH1bt[:], sAct[:, tt * D:(tt + 1) * D],
                                                   AF.Copy, accum_out=sStat[:, 0:1]))
            _w("vector", "sACT", sm)
            dve(lambda e: e.tensor_scalar_mul(sStat[:, 2:3], sStat[:, 0:1], 1.0 / D))
            m2 = dve(lambda e: e.tensor_scalar_mul(sStat[:, 3:4], sStat[:, 0:1], -1.0 / D))
            _w("vector", "sDVE", m2)
            m3 = dve(lambda e: e.tensor_tensor(sStat[:, 4:5], sStat[:, 2:3], sStat[:, 2:3], ALU.mult))
            m4 = dve(lambda e: e.tensor_scalar_mul(sStat[:, 5:6], sStat[:, 1:2], 1.0 / D))
            _w("vector", "sDVE", m4)
            st = dve(lambda e: e.tensor_tensor(sStat[:, 5:6], sStat[:, 5:6], sStat[:, 4:5], ALU.subtract))
            _w("scalar", "sDVE", st)
            lnc = act(lambda e: e.activation(sStat[:, 6:7], sStat[:, 5:6], AF.Ln, bias=EPS))
            _w("scalar", "sACT", lnc)
            rs = act(lambda e: e.activation(sStat[:, 7:8], sStat[:, 6:7], AF.Exp, scale=-0.5))
            _w("vector", "sACT", rs)
            nm = dve(lambda e: e.tensor_tensor(sStat[:, 4:5], sStat[:, 3:4], sStat[:, 7:8], ALU.mult))
            _w("scalar", "sDVE", nm)
            apb = act(lambda e, tt=tt: e.activation(sH1bt[:], sAct[:, tt * D:(tt + 1) * D],
                                                    AF.Identity, bias=sStat[:, 4:5], scale=sStat[:, 7:8]))
            ap = act(lambda e, tt=tt: e.activation(sAct[:, tt * D:(tt + 1) * D], sAct[:, tt * D:(tt + 1) * D],
                                                   AF.Identity, bias=sStat[:, 4:5], scale=sStat[:, 7:8]))
            return ap, apb

        for sl in range(2):
            if sl == 0:
                ogA_done = ogA_pre
            else:
                _w("sync", "sPE", outproj_pe_end)
                for hh in range(2):
                    load(lambda e, sl=sl, hh=hh: e.dma_start(
                        out=sOg[:].rearrange("p (r q) -> p r q", q=2048)[:, :, 512 * hh:512 * (hh + 1)],
                        in_=CCOA.ap().rearrange("(r hh p) t -> p r hh t", p=128, hh=2)[:, :, hh, 512 * sl:512 * (sl + 1)]), "qOG")
                ogA_done = n["qOG"]
            _w("scalar", "sCCb", 1)
            _w("scalar", "sPE", outproj_pe_end)
            for hh in range(2):
                loadq("scalar", lambda e, sl=sl, hh=hh: e.dma_start(
                    out=sOg[:].rearrange("p (r q) -> p r q", q=2048)[:, :, 1024 + 512 * hh:1024 + 512 * (hh + 1)],
                    in_=CCOB.ap().rearrange("(r hh p) t -> p r hh t", p=128, hh=2)[:, :, hh, 512 * sl:512 * (sl + 1)]), "qOGB")
            ogB_done = n["qOGB"]

            evac_dve = {}
            for oslab in range(8):
                _w("sync", "sDVE", sXs_user)
                for tt in range(4):
                    load(lambda e, sl=sl, tt=tt, oslab=oslab: e.dma_start(
                        out=sXs[0][:, tt * 512:(tt + 1) * 512],
                        in_=XR[512 * sl + 128 * tt:512 * sl + 128 * (tt + 1), 512 * oslab:512 * (oslab + 1)]), "qXS")
                xs_done = n["qXS"]
                for idx in range(32):
                    ic = IC_ORDER[idx]
                    if sl == 0 and oslab == 0 and idx < 8:
                        k, kb, cv = pre8[idx]
                    else:
                        k, kb, cv = stripe("o", ic, oslab)
                    _w("tensor", *cv)
                    if idx == 0:
                        _w("tensor", "qOG", ogA_done)
                        _w("tensor", "qC", p4c_total)
                    if idx == 16:
                        _w("tensor", "qOGB", ogB_done)
                    for tt in range(4):
                        if idx == 0:
                            _w("tensor", "sDVE", pAo_user[tt])
                        val = pe(lambda e, ic=ic, tt=tt, kb=kb, st=(idx == 0), sp=(idx == 31):
                                 e.matmul(pAo[tt][:], sOg[:, ic * 512 + tt * 128: ic * 512 + (tt + 1) * 128],
                                          sWb[kb][:], start=st, stop=sp), inc=(tt == 3))
                    stripe_user[k] = val
                outproj_pe_end = stripe_user[kglob[0] - 1]
                for tt in range(4):
                    _w("vector", "sPE", stripe_user[kglob[0] - 1])
                    _w("vector", "qXS", xs_done)
                    _w("vector", "qY", y_store_total)
                    ev = dve(lambda e, tt=tt, oslab=oslab: e.tensor_tensor(
                        sAct[:, tt * D + oslab * 512: tt * D + (oslab + 1) * 512],
                        pAo[tt][:], sXs[0][:, tt * 512:(tt + 1) * 512], ALU.add), inc=True)
                    pAo_user[tt] = ev
                    evac_dve[(oslab, tt)] = ev
                sXs_user = evac_dve[(oslab, 3)]

            h1t_done = 0
            h1t_last_pe = 0
            _w("tensor", "qID", 16)
            for tt in range(4):
                ap, apb = emit_ln(tt, evac_dve[(7, tt)], ready_pe=h1t_last_pe)
                _w("vector", "qB", 16)
                _w("vector", "sACT", ap)
                dve(lambda e, tt=tt: e.tensor_tensor(
                    sAct[:, tt * D:(tt + 1) * D], sAct[:, tt * D:(tt + 1) * D], sBrep[:], ALU.add))
                for ocb in range(8):
                    _w("tensor", "sACT", apb)
                    _w("tensor", "sDVE", pTr_user[ocb % 2])
                    last = None
                    for u in range(4):
                        oc = ocb * 4 + u
                        last = pe(lambda e, oc=oc, u=u, b=ocb % 2: e.transpose(
                            pTr[b][:, u * 128:(u + 1) * 128], sH1bt[:, oc * 128:(oc + 1) * 128], sIdent[:]),
                            inc=(u == 3))
                    _w("vector", "sPE", last)
                    ev = dve(lambda e, ocb=ocb, tt=tt, b=ocb % 2: e.tensor_copy(
                        sH1T[:].rearrange("p (a u x) -> p a u x", u=4, x=128)[:, ocb * 4:(ocb + 1) * 4, tt, :],
                        pTr[b][:, 0:512].rearrange("p (u x) -> p u x", u=4)))
                    pTr_user[ocb % 2] = ev
                    h1t_done = ev
                    h1t_last_pe = last

            _w("tensor", "sDVE", max(h1t_done, prg_prev))
            _w("tensor", "sDVE", pSm2_user)
            g_pe = pr_pe = None
            for ic in range(32):
                g_pe = pe(lambda e, ic=ic: e.matmul(pSmall[0:4, :], sWrTb[:, ic * 4:ic * 4 + 4],
                                                    sH1T[:, ic * 512:(ic + 1) * 512],
                                                    start=(ic == 0), stop=(ic == 31)), inc=(ic == 31))
            for ic in range(32):
                pr_pe = pe(lambda e, ic=ic: e.matmul(pSmall2[:], sAtb[:, ic * 128:(ic + 1) * 128],
                                                     sH1T[:, ic * 512:(ic + 1) * 512],
                                                     start=(ic == 0), stop=(ic == 31)), inc=(ic == 31))
            _w("scalar", "sPE", g_pe)
            ge = act(lambda e: e.activation(sGeb[:], pSmall[0:4, :], AF.Exp), inc=True)
            _w("tensor", "sACT", ge)
            gs_pe = pe(lambda e: e.matmul(pSmall[0:4, :], sOnes[0:4, 0:4], sGeb[:], start=True, stop=True), inc=True)
            _w("scalar", "sPE", gs_pe)
            lng = act(lambda e: e.activation(sGl[:], pSmall[0:4, :], AF.Ln), inc=True)
            _w("scalar", "sACT", lng)
            r4 = act(lambda e: e.activation(sR4[:], sGl[:], AF.Exp, scale=-1.0), inc=True)
            _w("vector", "sACT", r4)
            gn = dve(lambda e: e.tensor_tensor(sGn[:], sGeb[:], sR4[:], ALU.mult), inc=True)
            _w("tensor", "sDVE", gn)
            _w("tensor", "qE8", 16)
            grep_pe = pe(lambda e: e.matmul(pSmall[:], sE8[:], sGn[:], start=True, stop=True), inc=True)
            _w("vector", "sPE", max(pr_pe, grep_pe))
            pr_ev = dve(lambda e: e.tensor_copy(sPr32[:], pSmall2[:]), inc=True)
            pSm2_user = pr_ev
            prg = dve(lambda e: e.tensor_tensor(sPrg[:], pSmall[:], sPr32[:], ALU.mult), inc=True)
            prg_prev = prg

            evac5 = {}
            for oslab in range(8):
                for ic in range(32):
                    k, kb, cv = stripe("t", ic, oslab)
                    _w("tensor", *cv)
                    if ic == 0:
                        _w("tensor", "sDVE", prg)
                    for tt in range(4):
                        if ic == 0:
                            _w("tensor", "sDVE", pAo_user[tt])
                        val = pe(lambda e, ic=ic, tt=tt, kb=kb, st=(ic == 0):
                                 e.matmul(pAo[tt][:], sH1T[:, ic * 512 + tt * 128: ic * 512 + (tt + 1) * 128],
                                          sWb[kb][:], start=st, stop=False), inc=(tt == 3))
                    stripe_user[k] = val
                lora_done = {}
                for tt in range(4):
                    lora_done[tt] = pe(lambda e, tt=tt, oslab=oslab:
                                       e.matmul(pAo[tt][:], sPrg[:, tt * 128:(tt + 1) * 128],
                                                sBm[:, 512 * oslab:512 * (oslab + 1)],
                                                start=False, stop=True), inc=True)
                stripe_user[kglob[0] - 1] = lora_done[3]
                for tt in range(4):
                    _w("vector", "sPE", lora_done[tt])
                    ev = dve(lambda e, tt=tt, oslab=oslab: e.tensor_tensor(
                        sAct[:, tt * D + oslab * 512: tt * D + (oslab + 1) * 512],
                        pAo[tt][:], sAct[:, tt * D + oslab * 512: tt * D + (oslab + 1) * 512], ALU.add), inc=True)
                    pAo_user[tt] = ev
                    evac5[(oslab, tt)] = ev

            for tt in range(4):
                ap, apb = emit_ln(tt, evac5[(7, tt)])
                _w("scalar", "sACT", ap)
                store(lambda e, sl=sl, tt=tt: e.dma_start(
                    out=Y[512 * sl + 128 * tt: 512 * sl + 128 * (tt + 1), :],
                    in_=sAct[:, tt * D:(tt + 1) * D]), "qY")
            y_store_total = n["qY"]

    wait_all_stores("scalar")

    with contextlib.ExitStack() as emit_stack:
        sems = {}
        for name in ALL_SEMS:
            sems[name] = emit_stack.enter_context(nc.semaphore(name))
        block = emit_stack.enter_context(nc.Block())

        def mk(e):
            def emitfn(eng):
                for op in OPS[e]:
                    if op[0] == "w":
                        eng.wait_ge(sems[op[1]], op[2])
                    else:
                        ins = op[1](eng)
                        if op[2] is not None:
                            ins.then_inc(sems[op[2]], op[3])
            return emitfn

        block.sync(mk("sync"))
        block.tensor(mk("tensor"))
        block.vector(mk("vector"))
        block.scalar(mk("scalar"))
        block.gpsimd(mk("gpsimd"))

    es.close()
    return nc


def _prep_inputs(inputs):
    x = np.asarray(inputs["L0_attn_hidden_states"])[0]
    wq = np.asarray(inputs["L0_attn_wq"])
    wk = np.asarray(inputs["L0_attn_wk"])
    wv = np.asarray(inputs["L0_attn_wv"])
    wo = np.asarray(inputs["L0_attn_wo"])
    W = np.asarray(inputs["L0_ffn_W"])
    b = np.asarray(inputs["L0_ffn_b"])
    A = np.asarray(inputs["L0_ffn_A"])
    Bm = np.asarray(inputs["L0_ffn_Bm"])
    Wr = np.asarray(inputs["L0_ffn_Wr"])

    xT = np.ascontiguousarray(x.T)
    wt = np.ascontiguousarray(W.T)
    brep = np.ascontiguousarray(np.tile(b[None, :].astype(np.float32), (128, 1)))

    Apad = np.zeros((128, D), np.float32)
    for e in range(4):
        Apad[32 * e:32 * e + 8] = A[e]
    atm = np.ascontiguousarray(Apad.T.reshape(32, 128, 128).transpose(1, 0, 2).reshape(128, 32 * 128))
    bmt = np.zeros((128, D), np.float32)
    for e in range(4):
        bmt[32 * e:32 * e + 8] = Bm[e].T
    wrt = np.ascontiguousarray(Wr.T.reshape(32, 128, 4).transpose(1, 0, 2).reshape(128, 128))

    inv = 1.0 / 10000.0 ** (np.arange(0, HD, 2, dtype=np.float64) / HD)
    pos = np.arange(S, dtype=np.float64)
    fr = pos[:, None] * inv[None, :]
    emb = np.concatenate([fr, fr], axis=1)
    cosT = np.cos(emb).T.astype(np.float32)
    sinT = np.sin(emb).T.astype(np.float32)
    cs2 = np.empty((128, 2 * S), np.float32)
    for ts in range(NSLAB):
        cs2[:, 1024 * ts:1024 * ts + 512] = cosT[:, 512 * ts:512 * (ts + 1)]
        cs2[:, 1024 * ts + 512:1024 * (ts + 1)] = sinT[:, 512 * ts:512 * (ts + 1)]

    msk = np.zeros((128, 4 * QP), BF)
    ff = np.arange(QP)
    pp = np.arange(128)
    for dv in range(4):
        msk[:, dv * QP:(dv + 1) * QP] = (ff[None, :] >= (128 * dv + pp[:, None])).astype(BF)

    onesb = np.ones((128, 128), BF)
    ident = np.eye(128, dtype=np.float32).astype(BF)
    e8p = np.zeros((4, 128), BF)
    for e in range(4):
        e8p[e, 32 * e:32 * e + 8] = BF(2.0)
    cv = np.array([CEXP, SCL, -1.0, -0.5, EPS, 1.0, 1.0 / D, -1.0 / D, 0.0, 2.0, 0.5, -2.0],
                  np.float32)
    consts = np.tile(np.concatenate([cv, np.zeros(4, np.float32)]), (128, 1))

    def wslice(w, c):
        ws = w[:, FPC * c:FPC * (c + 1)]
        return np.ascontiguousarray(ws.reshape(32, 128, FPC).transpose(1, 0, 2).reshape(128, 32 * FPC))

    in_maps = []
    for c in range(NCORE):
        in_maps.append({
            "xt": xT,
            "xr": np.ascontiguousarray(x[TPC * c:TPC * (c + 1), :]),
            "wq": wslice(wq, c),
            "wk": wslice(wk, c),
            "wv": wslice(wv, c),
            "wo": wo,
            "wt": wt,
            "brep": brep,
            "atm": atm,
            "bmt": bmt,
            "wrt": wrt,
            "cs2": cs2,
            "msk": msk,
            "onesb": onesb,
            "ident": ident,
            "e8p": e8p,
            "consts": consts,
        })
    return in_maps


def kernel(**inputs):
    nc = build_program()
    in_maps = _prep_inputs(inputs)
    res = run_bass_kernel_spmd(nc, in_maps, core_ids=list(range(NCORE)))
    out = np.concatenate([np.asarray(res.results[c]["y"]) for c in range(NCORE)], axis=0)
    return out.reshape(1, S, D).astype(np.float32)


# version 14
# speedup vs baseline: 1.0250x; 1.0250x over previous
import os
import math
import contextlib

os.environ.setdefault("TRN_ALLOW_ICI_ALL_TO_ALL", "1")

import numpy as np
import ml_dtypes

import concourse.bass as bass
import concourse.mybir as mybir
from concourse.bass_utils import run_bass_kernel_spmd
from concourse.aot_env import allow_ici_all_to_all

F32 = mybir.dt.float32
BF16 = mybir.dt.bfloat16
AF = mybir.ActivationFunctionType
ALU = mybir.AluOpType
BF = ml_dtypes.bfloat16

NCORE = 8
S = 8192
D = 4096
HD = 128
HPC = 4
FPC = HPC * HD
TPC = S // NCORE
SLAB = 512
NSLAB = S // SLAB
QP = 512
NPAN = S // QP
SCL = 1.0 / math.sqrt(HD)
CEXP = -10.0
EPS = 1e-5
NXB = 4

ENGINES = ("sync", "tensor", "vector", "scalar", "gpsimd")
DMA_SEMS = (["qC", "qCN", "qON", "qID", "qE8", "qM", "qB", "qWQ", "qWK", "qWV", "qCSa", "qCSb", "qKV", "qKVb", "qQa", "qQb", "qOG", "qOGB", "qXS", "qY", "qPCa", "qPCb"]
            + [f"qX{i}" for i in range(NXB)]
            + ["qR0", "qR1"] + [f"qSV{i}" for i in range(4)]
            + ["qO0", "qO1"] + [f"qW{i}" for i in range(8)])
ALL_SEMS = DMA_SEMS + ["sPE", "sDVE", "sACT", "sPOOL", "sCCa", "sCCb", "sBAR"]


def build_program():
    nc = bass.Bass("TRN2", target_bir_lowering=False, debug=False)

    XT = nc.dram_tensor("xt", [D, S], F32, kind="ExternalInput").ap()
    XR = nc.dram_tensor("xr", [TPC, D], F32, kind="ExternalInput").ap()
    WQ = nc.dram_tensor("wq", [128, 32 * FPC], F32, kind="ExternalInput").ap()
    WK = nc.dram_tensor("wk", [128, 32 * FPC], F32, kind="ExternalInput").ap()
    WV = nc.dram_tensor("wv", [128, 32 * FPC], F32, kind="ExternalInput").ap()
    WO = nc.dram_tensor("wo", [D, D], F32, kind="ExternalInput").ap()
    WT = nc.dram_tensor("wt", [D, D], F32, kind="ExternalInput").ap()
    BREP = nc.dram_tensor("brep", [128, D], F32, kind="ExternalInput").ap()
    ATM = nc.dram_tensor("atm", [128, 32 * 128], F32, kind="ExternalInput").ap()
    BMT = nc.dram_tensor("bmt", [128, D], F32, kind="ExternalInput").ap()
    WRT = nc.dram_tensor("wrt", [128, 128], F32, kind="ExternalInput").ap()
    CS2 = nc.dram_tensor("cs2", [128, 2 * S], F32, kind="ExternalInput").ap()
    MSK = nc.dram_tensor("msk", [128, 4 * QP], BF16, kind="ExternalInput").ap()
    ONESB = nc.dram_tensor("onesb", [128, 128], BF16, kind="ExternalInput").ap()
    IDENT = nc.dram_tensor("ident", [128, 128], BF16, kind="ExternalInput").ap()
    E8P = nc.dram_tensor("e8p", [4, 128], BF16, kind="ExternalInput").ap()
    CONSTS = nc.dram_tensor("consts", [128, 16], F32, kind="ExternalInput").ap()
    Y = nc.dram_tensor("y", [TPC, D], F32, kind="ExternalOutput").ap()

    QTd = nc.dram_tensor("qtd", [FPC, S], BF16)
    KTd = nc.dram_tensor("ktd", [FPC, S], BF16)
    Vd = nc.dram_tensor("vd", [S, FPC], BF16)
    CCIA = nc.dram_tensor("ccia", [D // 2, TPC], BF16)
    CCIB = nc.dram_tensor("ccib", [D // 2, TPC], BF16)
    CCOA = nc.dram_tensor("ccoa", [D // 2, TPC], BF16)
    CCOB = nc.dram_tensor("ccob", [D // 2, TPC], BF16)
    WOB = nc.dram_tensor("wob", [8 * 32 * 128, 512], BF16)
    WTB = nc.dram_tensor("wtb", [8 * 32 * 128, 512], BF16)

    OPS = {e: [] for e in ENGINES}
    n = {"PE": 0, "DVE": 0, "ACT": 0, "POOL": 0, "BAR": 0}
    for s_ in DMA_SEMS:
        n[s_] = 0
    hwm = {e: {} for e in ENGINES}

    def _w(e, sem, val):
        if val is None or val <= 0:
            return
        if hwm[e].get(sem, 0) >= val:
            return
        hwm[e][sem] = val
        OPS[e].append(("w", sem, val))

    def _i(e, fn, sem=None, inc=1):
        OPS[e].append(("i", fn, sem, inc))

    def load(fn, sem):
        n[sem] += 16
        _i("sync", fn, sem, 16)
        return (sem, n[sem])

    def gload(fn, sem):
        n[sem] += 16
        _i("gpsimd", fn, sem, 16)
        return (sem, n[sem])

    def loadq(engq, fn, sem):
        n[sem] += 16
        _i(engq, fn, sem, 16)
        return (sem, n[sem])

    def store(fn, sem):
        n[sem] += 16
        _i("scalar", fn, sem, 16)
        return (sem, n[sem])

    def pe(fn, inc=False):
        if inc:
            n["PE"] += 1
            _i("tensor", fn, "sPE", 1)
            return n["PE"]
        _i("tensor", fn)
        return None

    def dve(fn, inc=True):
        _w("vector", "sDVE", n["DVE"])
        n["DVE"] += 1
        _i("vector", fn, "sDVE", 1)
        return n["DVE"]

    def act(fn, inc=True):
        _w("scalar", "sACT", n["ACT"])
        n["ACT"] += 1
        _i("scalar", fn, "sACT", 1)
        return n["ACT"]

    def pool(fn, inc=True):
        _w("gpsimd", "sPOOL", n["POOL"])
        n["POOL"] += 1
        _i("gpsimd", fn, "sPOOL", 1)
        return n["POOL"]

    def barrier():
        n["BAR"] += 1
        tgt = 5 * n["BAR"]
        for e in ENGINES:
            _i(e, lambda eng: eng.nop(), "sBAR", 1)
        for e in ENGINES:
            hwm[e]["sBAR"] = tgt
            OPS[e].append(("w", "sBAR", tgt))

    def wait_all_stores(e):
        for s_ in ("qR0", "qR1", "qSV0", "qSV1", "qSV2", "qSV3", "qO0", "qO1", "qY"):
            _w(e, s_, n[s_])

    es = contextlib.ExitStack()

    sconst = es.enter_context(nc.sbuf_tensor("sconst", [128, 16], F32))
    sOnes = es.enter_context(nc.sbuf_tensor("sOnes", [128, 128], BF16))
    sIdent = es.enter_context(nc.sbuf_tensor("sIdent", [128, 128], BF16))
    sE8 = es.enter_context(nc.sbuf_tensor("sE8", [4, 128], BF16))
    sAtb = es.enter_context(nc.sbuf_tensor("sAtb", [128, 32 * 128], BF16))
    sWrTb = es.enter_context(nc.sbuf_tensor("sWrTb", [128, 128], BF16))
    sBm = es.enter_context(nc.sbuf_tensor("sBm", [128, D], BF16))
    sBrep = es.enter_context(nc.sbuf_tensor("sBrep", [128, D], F32))
    sWb = [es.enter_context(nc.sbuf_tensor(f"sWb{i}", [128, 512], BF16)) for i in range(8)]
    sOg = es.enter_context(nc.sbuf_tensor("sOg", [128, 32 * 512], BF16))

    stripe_user = {}
    kglob = [0]

    def stripe(which, ic, oslab):
        k = kglob[0]
        kb = k % 8
        eng = "sync" if kb % 2 == 0 else "gpsimd"
        Wb, pcs = (WOB, "qPCa") if which == "o" else (WTB, "qPCb")
        _w(eng, "sPE", stripe_user.get(k - 8, 0))
        _w(eng, pcs, 32 * 16)
        r0 = (oslab * 32 + ic) * 128
        lw = loadq(eng, lambda e, r0=r0, kb=kb, Wb=Wb: e.dma_start(
            out=sWb[kb][:], in_=Wb.ap()[r0:r0 + 128, :]), f"qW{kb}")
        kglob[0] += 1
        return k, kb, lw

    CONST_VALS = [CEXP, SCL, -1.0, -0.5, EPS, 1.0, 1.0 / D, -1.0 / D, 0.0, 2.0, 0.5, -2.0]
    for j, v in enumerate(CONST_VALS):
        nc.const_aps.register(F32, float(v), sconst[:, j:j + 1])

    with contextlib.ExitStack() as p1:
        swq = p1.enter_context(nc.sbuf_tensor("swq", [128, 32 * FPC], BF16))
        swk = p1.enter_context(nc.sbuf_tensor("swk", [128, 32 * FPC], BF16))
        sxb = [p1.enter_context(nc.sbuf_tensor(f"sxb{i}", [128, 4 * SLAB], BF16)) for i in range(NXB)]
        scs = [p1.enter_context(nc.sbuf_tensor(f"scs{i}", [128, 2 * SLAB], F32)) for i in range(2)]
        sQf = [p1.enter_context(nc.sbuf_tensor(f"sQf{i}", [128, SLAB], F32)) for i in range(8)]
        sRot = [p1.enter_context(nc.sbuf_tensor(f"sRot{i}", [128, SLAB], F32)) for i in range(4)]
        sQc = [p1.enter_context(nc.sbuf_tensor(f"sQc{i}", [128, SLAB], F32)) for i in range(2)]
        sQs = [p1.enter_context(nc.sbuf_tensor(f"sQs{i}", [128, SLAB], F32)) for i in range(2)]
        sQr = [p1.enter_context(nc.sbuf_tensor(f"sQr{i}", [128, SLAB], BF16)) for i in range(2)]
        pQK = [p1.enter_context(nc.psum_tensor(f"pQK{j}", [128, SLAB], F32)) for j in range(8)]

        load(lambda e: e.dma_start(out=sconst[:], in_=CONSTS), "qCN")
        load(lambda e: e.dma_start(out=sOnes[:], in_=ONESB), "qON")
        load(lambda e: e.dma_start(out=sIdent[:], in_=IDENT), "qID")
        load(lambda e: e.dma_start(out=sE8[:], in_=E8P), "qE8")
        load(lambda e: e.dma_start(out=sBrep[:], in_=BREP), "qB")
        gload(lambda e: e.dma_start(out=sAtb[:], in_=ATM), "qC")
        gload(lambda e: e.dma_start(out=sWrTb[:], in_=WRT), "qC")
        gload(lambda e: e.dma_start(out=sBm[:], in_=BMT), "qC")
        p4c_total = n["qC"]
        wq_l = gload(lambda e: e.dma_start(out=swq[:], in_=WQ), "qWQ")
        wk_l = gload(lambda e: e.dma_start(out=swk[:], in_=WK), "qWK")

        xload = {}
        xconv = {}
        icgrp = {}
        bankfree = {}
        rotdone = {}
        st_qk = {}
        cs_load = {}

        def p1a_finish_slab(ts):
            cb = ts % 2
            for j in range(8):
                _w("scalar", "sPE", icgrp[ts * 8 + 7])
                _w("scalar", "sDVE", rotdone.get((ts - 1, j), 0))
                bankfree[(ts, j)] = act(lambda e, j=j: e.copy(sQf[j][:], pQK[j][:]))
                _w("vector", "sACT", bankfree[(ts, j)])
                pj4 = (ts, j - 4) if j >= 4 else (ts - 1, j + 4)
                _w("vector", "sDVE", rotdone.get(pj4, 0))
                dve(lambda e, j=j, b=j % 4: e.tensor_scalar_mul(sRot[b][0:64, :], sQf[j][64:128, :], -1.0))
                rp = dve(lambda e, j=j, b=j % 4: e.tensor_copy(sRot[b][64:128, :], sQf[j][0:64, :]))
                _w("vector", *cs_load[ts])
                pj = (ts, j - 2) if j >= 2 else (ts - 1, j + 6)
                if pj in st_qk:
                    _w("vector", *st_qk[pj])
                dve(lambda e, j=j, b=j % 2, cb=cb: e.tensor_tensor(sQc[b][:], sQf[j][:], scs[cb][:, 0:SLAB], ALU.mult))
                dve(lambda e, b=j % 4, bb=j % 2, cb=cb: e.tensor_tensor(sQs[bb][:], sRot[b][:], scs[cb][:, SLAB:2 * SLAB], ALU.mult))
                rotdone[(ts, j)] = dve(lambda e, b=j % 2: e.tensor_tensor(sQr[b][:], sQc[b][:], sQs[b][:], ALU.add))
                _w("scalar", "sDVE", rotdone[(ts, j)])
                dst = QTd if j < 4 else KTd
                oc = j % 4
                st_qk[(ts, j)] = store(lambda e, dst=dst, oc=oc, tp=SLAB * ts, b=j % 2: e.dma_start(
                    out=dst.ap()[128 * oc:128 * (oc + 1), tp:tp + SLAB], in_=sQr[b][:]),
                    "qR0" if j % 2 == 0 else "qR1")

        for ts in range(NSLAB):
            t0 = SLAB * ts
            _w("sync", "sDVE", rotdone.get((ts - 2, 7), 0))
            cs_load[ts] = load(lambda e, ts=ts, b=ts % 2: e.dma_start(
                out=scs[b][:], in_=CS2[:, 1024 * ts:1024 * (ts + 1)]),
                "qCSa" if ts % 2 == 0 else "qCSb")
            def p1a_loadconv(gq):
                g = ts * 8 + gq
                _w("gpsimd", "sPE", icgrp.get(g - NXB, 0))
                xload[g] = gload(lambda e, t0=t0, gq=gq, b=g % NXB: e.dma_start(
                    out=sxb[b][:].rearrange("p (c t) -> p c t", c=4),
                    in_=XT.rearrange("(c p) t -> p c t", p=128)[:, 4 * gq:4 * (gq + 1), t0:t0 + SLAB]),
                    f"qX{g % NXB}")
                xconv[g] = xload[g]

            for gq in range(2):
                p1a_loadconv(gq)
            if ts >= 1:
                p1a_finish_slab(ts - 1)
            if ts == 0:
                _w("tensor", *wq_l)
                _w("tensor", *wk_l)
            for ic in range(32):
                g = ts * 8 + ic // 4
                if ic >= 8 and ic % 4 == 0:
                    p1a_loadconv(ic // 4)
                _w("tensor", *xconv[g])
                for j in range(8):
                    if ic == 0:
                        _w("tensor", "sACT", bankfree.get((ts - 1, j), 0))
                    src = swq if j < 4 else swk
                    oc = j % 4
                    val = pe(lambda e, j=j, ic=ic, src=src, oc=oc, b=g % NXB, st=(ic == 0), sp=(ic == 31):
                             e.matmul(pQK[j][:], src[:, ic * FPC + oc * 128: ic * FPC + (oc + 1) * 128],
                                      sxb[b][:, (ic % 4) * SLAB:(ic % 4 + 1) * SLAB], start=st, stop=sp),
                             inc=(j == 7))
                    if j == 7:
                        icgrp[ts * 8 + ic // 4] = val

        p1a_finish_slab(NSLAB - 1)

    barrier()

    with contextlib.ExitStack() as p1b:
        swv = p1b.enter_context(nc.sbuf_tensor("swv", [128, 32 * FPC], BF16))
        sxbB = [p1b.enter_context(nc.sbuf_tensor(f"sxbB{i}", [128, 4 * SLAB], BF16)) for i in range(NXB)]
        sVe = [p1b.enter_context(nc.sbuf_tensor(f"sVe{i}", [128, FPC], BF16)) for i in range(4)]
        pV = [p1b.enter_context(nc.psum_tensor(f"pV{j}", [128, FPC], F32)) for j in range(8)]

        wv_l = gload(lambda e: e.dma_start(out=swv[:], in_=WV), "qWV")

        xloadB, xconvB, icgrpB, bfreeB, stV = {}, {}, {}, {}, {}

        def p1b_evacs(ts):
            half = (ts % 2) * 4
            for tt in range(4):
                _w("scalar", "sPE", icgrpB[ts * 8 + 7])
                if (ts - 1, tt) in stV:
                    _w("scalar", *stV[(ts - 1, tt)])
                bfreeB[(ts, tt)] = act(lambda e, tt=tt, half=half: e.copy(sVe[tt][:], pV[half + tt][:]))
                _w("scalar", "sACT", bfreeB[(ts, tt)])
                stV[(ts, tt)] = store(lambda e, tp=SLAB * ts, tt=tt: e.dma_start(
                    out=Vd.ap()[tp + 128 * tt: tp + 128 * (tt + 1), :], in_=sVe[tt][:]), f"qSV{tt}")

        for ts in range(NSLAB):
            t0 = SLAB * ts
            def p1b_loadconv(gq):
                g = ts * 8 + gq
                _w("gpsimd", "sPE", icgrpB.get(g - NXB, 0))
                xloadB[g] = gload(lambda e, t0=t0, gq=gq, b=g % NXB: e.dma_start(
                    out=sxbB[b][:].rearrange("p (c t) -> p c t", c=4),
                    in_=XT.rearrange("(c p) t -> p c t", p=128)[:, 4 * gq:4 * (gq + 1), t0:t0 + SLAB]),
                    f"qX{g % NXB}")
                xconvB[g] = xloadB[g]

            for gq in range(2):
                p1b_loadconv(gq)
            if ts >= 1:
                p1b_evacs(ts - 1)
            if ts == 0:
                _w("tensor", *wv_l)
            half = (ts % 2) * 4
            for ic in range(32):
                g = ts * 8 + ic // 4
                if ic >= 8 and ic % 4 == 0:
                    p1b_loadconv(ic // 4)
                _w("tensor", *xconvB[g])
                for tt in range(4):
                    if ic == 0:
                        _w("tensor", "sACT", bfreeB.get((ts - 2, tt), 0))
                    val = pe(lambda e, tt=tt, ic=ic, b=g % NXB, st=(ic == 0), sp=(ic == 31), half=half:
                             e.matmul(pV[half + tt][:], sxbB[b][:, (ic % 4) * SLAB + tt * 128:(ic % 4) * SLAB + (tt + 1) * 128],
                                      swv[:, ic * FPC:(ic + 1) * FPC], start=st, stop=sp),
                             inc=(tt == 3))
                    if tt == 3:
                        icgrpB[g] = val
        p1b_evacs(NSLAB - 1)

    barrier()

    with contextlib.ExitStack() as p2:
        sMask = p2.enter_context(nc.sbuf_tensor("sMask", [128, 4 * QP], BF16))
        skT = [p2.enter_context(nc.sbuf_tensor(f"skT{i}", [128, S], BF16)) for i in range(2)]
        sv = [p2.enter_context(nc.sbuf_tensor(f"sv{i}", [128, S], BF16)) for i in range(2)]
        sqTp = [p2.enter_context(nc.sbuf_tensor(f"sqTp{i}", [128, QP], BF16)) for i in range(2)]
        sP = [p2.enter_context(nc.sbuf_tensor(f"sP{i}", [128, QP], BF16)) for i in range(4)]
        sPt = [p2.enter_context(nc.sbuf_tensor(f"sPt{i}", [128, QP], BF16)) for i in range(2)]
        sRec = [p2.enter_context(nc.sbuf_tensor(f"sRec{i}", [128, QP], F32)) for i in range(2)]
        sLn2b = [p2.enter_context(nc.sbuf_tensor(f"sLn2b{i}", [128, QP], F32)) for i in range(2)]
        sOn = [p2.enter_context(nc.sbuf_tensor(f"sOn{i}", [128, QP], BF16)) for i in range(2)]
        pS = [p2.enter_context(nc.psum_tensor(f"pS{i}", [128, QP], F32)) for i in range(3)]
        pO = [p2.enter_context(nc.psum_tensor(f"pO{i}", [128, QP], F32)) for i in range(2)]
        pSm = [p2.enter_context(nc.psum_tensor(f"pSm{i}", [128, QP], F32)) for i in range(2)]

        for src, dst, pcs in ((WO, WOB, "qPCa"), (WT, WTB, "qPCb")):
            for ch in range(32):
                gload(lambda e, src=src, dst=dst, ch=ch: e.dma_start(
                    out=dst.ap().rearrange("(o r) c -> o r c", o=8)[:, ch * 128:(ch + 1) * 128, :],
                    in_=src[128 * ch:128 * (ch + 1), :].rearrange("p (o c) -> o p c", o=8)), pcs)

        for s_ in ("qR0", "qR1", "qSV0", "qSV1", "qSV2", "qSV3"):
            _w("sync", s_, n[s_])
        lmask = load(lambda e: e.dma_start(out=sMask[:], in_=MSK), "qM")
        _w("scalar", "qCN", 16)
        _w("vector", *lmask)
        _w("tensor", "qON", 16)

        qload = {}
        kv_done = {}
        qbuf_user = {}
        pS_user = {}
        sP_user = {}
        sPt_user = {}
        pO_user = {}
        pSm_user = {}
        sOn_user = {}
        head_pe_end = 0

        head_ends = {}

        def load_head_kv(h):
            hb = h % 2
            sem = "qKV" if hb == 0 else "qKVb"
            load(lambda e, h=h, hb=hb: e.dma_start(out=skT[hb][:], in_=KTd.ap()[128 * h:128 * (h + 1), :]), sem)
            for g in range(8):
                load(lambda e, h=h, g=g, hb=hb: e.dma_start(
                    out=sv[hb][:, g * 1024:(g + 1) * 1024].rearrange("p (kt d) -> p kt d", d=128),
                    in_=Vd.ap().rearrange("(kt p) (hh d) -> p kt hh d", p=128, d=128)[:, 8 * g:8 * (g + 1), h, :]), sem)
            kv_done[h] = (sem, n[sem])

        load_head_kv(0)
        qo_after_h1 = None
        for h in range(HPC):
            hb = h % 2
            if h + 1 < HPC:
                _w("sync", "sPE", head_ends.get(h - 1, 0))
                load_head_kv(h + 1)

            for p in range(NPAN):
                nkt = 4 * (p + 1)
                qb = p % 2
                ob = p % 2
                _w("sync", "sPE", qbuf_user.get(qb, 0))
                qload[(h, p)] = load(lambda e, h=h, p=p, qb=qb: e.dma_start(
                    out=sqTp[qb][:], in_=QTd.ap()[128 * h:128 * (h + 1), QP * p:QP * (p + 1)]),
                    "qQa" if qb == 0 else "qQb")

                exp_done = {}
                last_sum = 0

                def emit_score(kt):
                    sb = kt % 3
                    if kt == 0:
                        _w("tensor", *qload[(h, p)])
                        _w("tensor", *kv_done[h])
                    _w("tensor", "sACT", pS_user.get(sb, 0))
                    v = pe(lambda e, kt=kt, sb=sb, qb=qb, hb=hb: e.matmul(
                        pS[sb][:], skT[hb][:, kt * 128:(kt + 1) * 128], sqTp[qb][:], start=True, stop=True), inc=True)
                    qbuf_user[qb] = v
                    return v

                peS = {0: emit_score(0)}
                if nkt > 1:
                    peS[1] = emit_score(1)
                for kt in range(nkt):
                    sb = kt % 3
                    pb = kt % 4
                    diag = (kt >= 4 * p)
                    _w("scalar", "sPE", peS[kt])
                    if diag:
                        _w("scalar", "sDVE", sPt_user.get(kt % 2, 0))
                        tgt = sPt[kt % 2]
                    else:
                        _w("scalar", "sPE", sP_user.get(pb, 0))
                        tgt = sP[pb]
                    exp_done[kt] = act(lambda e, sb=sb, tgt=tgt: e.activation(
                        tgt[:], pS[sb][:], AF.Exp, bias=CEXP, scale=SCL), inc=True)
                    pS_user[sb] = exp_done[kt]
                    if diag:
                        dv = kt - 4 * p
                        _w("vector", "sACT", exp_done[kt])
                        _w("vector", "sPE", sP_user.get(pb, 0))
                        mk_ = dve(lambda e, pb=pb, tb=kt % 2, dv=dv: e.tensor_tensor(
                            sP[pb][:], sPt[tb][:], sMask[:, dv * QP:(dv + 1) * QP], ALU.mult))
                        sPt_user[kt % 2] = mk_
                    if kt + 2 < nkt:
                        peS[kt + 2] = emit_score(kt + 2)
                    if diag:
                        _w("tensor", "sDVE", mk_)
                    else:
                        _w("tensor", "sACT", exp_done[kt])
                    if kt == 0:
                        _w("tensor", "sDVE", pO_user.get(ob, 0))
                        _w("tensor", "sACT", pSm_user.get(ob, 0))
                    pe(lambda e, kt=kt, pb=pb, ob=ob, sp=(kt == nkt - 1), hb=hb: e.matmul(
                        pO[ob][:], sv[hb][:, kt * 128:(kt + 1) * 128], sP[pb][:],
                        start=(kt == 0), stop=sp))
                    last_sum = pe(lambda e, kt=kt, pb=pb, ob=ob, sp=(kt == nkt - 1): e.matmul(
                        pSm[ob][:], sOnes[:], sP[pb][:],
                        start=(kt == 0), stop=sp), inc=True)
                    sP_user[pb] = last_sum
                head_pe_end = last_sum
                head_ends[h] = last_sum
                _w("scalar", "sPE", last_sum)
                _w("scalar", "sACT", pSm_user.get(ob, 0))
                lnc = act(lambda e, ob=ob: e.activation(sLn2b[ob][:], pSm[ob][:], AF.Ln), inc=True)
                _w("scalar", "sACT", lnc)
                epi = act(lambda e, ob=ob: e.activation(sRec[ob][:], sLn2b[ob][:], AF.Exp, scale=-1.0), inc=True)
                pSm_user[ob] = epi
                _w("vector", "sACT", epi)
                if ob in sOn_user:
                    _w("vector", *sOn_user[ob])
                nrm = dve(lambda e, ob=ob: e.tensor_tensor(sOn[ob][:], pO[ob][:], sRec[ob][:], ALU.mult), inc=True)
                pO_user[ob] = nrm
                _w("scalar", "sDVE", nrm)
                cct = CCIA if h < 2 else CCIB
                r0 = 256 * (p // 2) + 128 * (h % 2)
                c0 = 512 * (p % 2)
                sOn_user[ob] = store(lambda e, cct=cct, r0=r0, c0=c0, ob=ob: e.dma_start(
                    out=cct.ap()[r0:r0 + 128, c0:c0 + QP], in_=sOn[ob][:]), "qO0" if ob == 0 else "qO1")
            if h == 1:
                qo_after_h1 = (n["qO0"], n["qO1"])

    _w("gpsimd", "qO0", qo_after_h1[0])
    _w("gpsimd", "qO1", qo_after_h1[1])
    with allow_ici_all_to_all():
        _i("gpsimd", lambda e: e.collective_compute(
            "AllToAll", ALU.bypass, replica_groups=[list(range(NCORE))],
            ins=[CCIA.ap().opt()], outs=[CCOA.ap().opt()]), "sCCa", 1)
    wait_all_stores("gpsimd")
    with allow_ici_all_to_all():
        _i("gpsimd", lambda e: e.collective_compute(
            "AllToAll", ALU.bypass, replica_groups=[list(range(NCORE))],
            ins=[CCIB.ap().opt()], outs=[CCOB.ap().opt()]), "sCCb", 1)

    _w("sync", "sCCa", 1)
    for hh in range(2):
        loadq("sync", lambda e, hh=hh: e.dma_start(
            out=sOg[:].rearrange("p (r q) -> p r q", q=2048)[:, :, 512 * hh:512 * (hh + 1)],
            in_=CCOA.ap().rearrange("(r hh p) t -> p r hh t", p=128, hh=2)[:, :, hh, 0:512]), "qOG")
    ogA_pre = n["qOG"]
    IC_ORDER = [ic for ic in range(32) if ic % 4 < 2] + [ic for ic in range(32) if ic % 4 >= 2]
    pre8 = [stripe("o", IC_ORDER[i], 0) for i in range(8)]

    barrier()

    with contextlib.ExitStack() as p4:
        sAct = p4.enter_context(nc.sbuf_tensor("sAct", [128, 4 * D], F32))
        sH1bt = p4.enter_context(nc.sbuf_tensor("sH1bt", [128, D], BF16))
        sH1T = p4.enter_context(nc.sbuf_tensor("sH1T", [128, 32 * 512], BF16))
        sXs = [p4.enter_context(nc.sbuf_tensor(f"sXs{i}", [128, 4 * 512], F32)) for i in range(1)]
        sStat = p4.enter_context(nc.sbuf_tensor("sStat", [128, 8], F32))
        sGeb = p4.enter_context(nc.sbuf_tensor("sGeb", [4, 512], BF16))
        sGl = p4.enter_context(nc.sbuf_tensor("sGl", [4, 512], BF16))
        sR4 = p4.enter_context(nc.sbuf_tensor("sR4", [4, 512], BF16))
        sGn = p4.enter_context(nc.sbuf_tensor("sGn", [4, 512], BF16))
        sPr32 = p4.enter_context(nc.sbuf_tensor("sPr32", [128, 512], F32))
        sPrg = p4.enter_context(nc.sbuf_tensor("sPrg", [128, 512], BF16))
        pAo = [p4.enter_context(nc.psum_tensor(f"pAo{i}", [128, 512], F32)) for i in range(4)]
        pTr = [p4.enter_context(nc.psum_tensor(f"pTr{i}", [128, 1024], BF16)) for i in range(2)]
        pSmall = p4.enter_context(nc.psum_tensor("pSmall", [128, 512], F32))
        pSmall2 = p4.enter_context(nc.psum_tensor("pSmall2", [128, 512], F32))

        pAo_user = {tt: 0 for tt in range(4)}
        sXs_user = 0
        pTr_user = {0: 0, 1: 0}
        y_store_total = 0
        outproj_pe_end = 0
        prg_prev = 0
        pSm2_user = 0

        def emit_ln(tt, ready_dve, ready_pe=0):
            _w("scalar", "sDVE", ready_dve)
            _w("scalar", "sPE", ready_pe)
            act(lambda e, tt=tt: e.activation(sH1bt[:], sAct[:, tt * D:(tt + 1) * D],
                                              AF.Square, accum_out=sStat[:, 1:2]))
            sm = act(lambda e, tt=tt: e.activation(sH1bt[:], sAct[:, tt * D:(tt + 1) * D],
                                                   AF.Copy, accum_out=sStat[:, 0:1]))
            _w("vector", "sACT", sm)
            dve(lambda e: e.tensor_scalar_mul(sStat[:, 2:3], sStat[:, 0:1], 1.0 / D))
            m2 = dve(lambda e: e.tensor_scalar_mul(sStat[:, 3:4], sStat[:, 0:1], -1.0 / D))
            _w("vector", "sDVE", m2)
            m3 = dve(lambda e: e.tensor_tensor(sStat[:, 4:5], sStat[:, 2:3], sStat[:, 2:3], ALU.mult))
            m4 = dve(lambda e: e.tensor_scalar_mul(sStat[:, 5:6], sStat[:, 1:2], 1.0 / D))
            _w("vector", "sDVE", m4)
            st = dve(lambda e: e.tensor_tensor(sStat[:, 5:6], sStat[:, 5:6], sStat[:, 4:5], ALU.subtract))
            _w("scalar", "sDVE", st)
            lnc = act(lambda e: e.activation(sStat[:, 6:7], sStat[:, 5:6], AF.Ln, bias=EPS))
            _w("scalar", "sACT", lnc)
            rs = act(lambda e: e.activation(sStat[:, 7:8], sStat[:, 6:7], AF.Exp, scale=-0.5))
            _w("vector", "sACT", rs)
            nm = dve(lambda e: e.tensor_tensor(sStat[:, 4:5], sStat[:, 3:4], sStat[:, 7:8], ALU.mult))
            _w("scalar", "sDVE", nm)
            apb = act(lambda e, tt=tt: e.activation(sH1bt[:], sAct[:, tt * D:(tt + 1) * D],
                                                    AF.Identity, bias=sStat[:, 4:5], scale=sStat[:, 7:8]))
            ap = act(lambda e, tt=tt: e.activation(sAct[:, tt * D:(tt + 1) * D], sAct[:, tt * D:(tt + 1) * D],
                                                   AF.Identity, bias=sStat[:, 4:5], scale=sStat[:, 7:8]))
            return ap, apb

        for sl in range(2):
            if sl == 0:
                ogA_done = ogA_pre
            else:
                _w("sync", "sPE", outproj_pe_end)
                for hh in range(2):
                    load(lambda e, sl=sl, hh=hh: e.dma_start(
                        out=sOg[:].rearrange("p (r q) -> p r q", q=2048)[:, :, 512 * hh:512 * (hh + 1)],
                        in_=CCOA.ap().rearrange("(r hh p) t -> p r hh t", p=128, hh=2)[:, :, hh, 512 * sl:512 * (sl + 1)]), "qOG")
                ogA_done = n["qOG"]
            _w("scalar", "sCCb", 1)
            _w("scalar", "sPE", outproj_pe_end)
            for hh in range(2):
                loadq("scalar", lambda e, sl=sl, hh=hh: e.dma_start(
                    out=sOg[:].rearrange("p (r q) -> p r q", q=2048)[:, :, 1024 + 512 * hh:1024 + 512 * (hh + 1)],
                    in_=CCOB.ap().rearrange("(r hh p) t -> p r hh t", p=128, hh=2)[:, :, hh, 512 * sl:512 * (sl + 1)]), "qOGB")
            ogB_done = n["qOGB"]

            evac_dve = {}
            for oslab in range(8):
                for idx in range(32):
                    ic = IC_ORDER[idx]
                    if sl == 0 and oslab == 0 and idx < 8:
                        k, kb, cv = pre8[idx]
                    else:
                        k, kb, cv = stripe("o", ic, oslab)
                    _w("tensor", *cv)
                    if idx == 0:
                        _w("tensor", "qOG", ogA_done)
                        _w("tensor", "qC", p4c_total)
                    if idx == 16:
                        _w("tensor", "qOGB", ogB_done)
                    for tt in range(4):
                        if idx == 0:
                            _w("tensor", "sDVE", pAo_user[tt])
                        val = pe(lambda e, ic=ic, tt=tt, kb=kb, st=(idx == 0), sp=(idx == 31):
                                 e.matmul(pAo[tt][:], sOg[:, ic * 512 + tt * 128: ic * 512 + (tt + 1) * 128],
                                          sWb[kb][:], start=st, stop=sp), inc=(tt == 3))
                    stripe_user[k] = val
                outproj_pe_end = stripe_user[kglob[0] - 1]
                _w("sync", "sDVE", sXs_user)
                for tt in range(4):
                    load(lambda e, sl=sl, tt=tt, oslab=oslab: e.dma_start(
                        out=sXs[0][:, tt * 512:(tt + 1) * 512],
                        in_=XR[512 * sl + 128 * tt:512 * sl + 128 * (tt + 1), 512 * oslab:512 * (oslab + 1)]), "qXS")
                xs_done = n["qXS"]
                for tt in range(4):
                    _w("vector", "sPE", stripe_user[kglob[0] - 1])
                    _w("vector", "qXS", xs_done)
                    _w("vector", "qY", y_store_total)
                    ev = dve(lambda e, tt=tt, oslab=oslab: e.tensor_tensor(
                        sAct[:, tt * D + oslab * 512: tt * D + (oslab + 1) * 512],
                        pAo[tt][:], sXs[0][:, tt * 512:(tt + 1) * 512], ALU.add), inc=True)
                    pAo_user[tt] = ev
                    evac_dve[(oslab, tt)] = ev
                sXs_user = evac_dve[(oslab, 3)]

            h1t_done = 0
            h1t_last_pe = 0
            _w("tensor", "qID", 16)
            for tt in range(4):
                ap, apb = emit_ln(tt, evac_dve[(7, tt)], ready_pe=h1t_last_pe)
                _w("vector", "qB", 16)
                _w("vector", "sACT", ap)
                dve(lambda e, tt=tt: e.tensor_tensor(
                    sAct[:, tt * D:(tt + 1) * D], sAct[:, tt * D:(tt + 1) * D], sBrep[:], ALU.add))
                for ocb in range(8):
                    _w("tensor", "sACT", apb)
                    _w("tensor", "sDVE", pTr_user[ocb % 2])
                    last = None
                    for u in range(4):
                        oc = ocb * 4 + u
                        last = pe(lambda e, oc=oc, u=u, b=ocb % 2: e.transpose(
                            pTr[b][:, u * 128:(u + 1) * 128], sH1bt[:, oc * 128:(oc + 1) * 128], sIdent[:]),
                            inc=(u == 3))
                    _w("vector", "sPE", last)
                    ev = dve(lambda e, ocb=ocb, tt=tt, b=ocb % 2: e.tensor_copy(
                        sH1T[:].rearrange("p (a u x) -> p a u x", u=4, x=128)[:, ocb * 4:(ocb + 1) * 4, tt, :],
                        pTr[b][:, 0:512].rearrange("p (u x) -> p u x", u=4)))
                    pTr_user[ocb % 2] = ev
                    h1t_done = ev
                    h1t_last_pe = last

            _w("tensor", "sDVE", max(h1t_done, prg_prev))
            _w("tensor", "sDVE", pSm2_user)
            g_pe = pr_pe = None
            for ic in range(32):
                g_pe = pe(lambda e, ic=ic: e.matmul(pSmall[0:4, :], sWrTb[:, ic * 4:ic * 4 + 4],
                                                    sH1T[:, ic * 512:(ic + 1) * 512],
                                                    start=(ic == 0), stop=(ic == 31)), inc=(ic == 31))
            for ic in range(32):
                pr_pe = pe(lambda e, ic=ic: e.matmul(pSmall2[:], sAtb[:, ic * 128:(ic + 1) * 128],
                                                     sH1T[:, ic * 512:(ic + 1) * 512],
                                                     start=(ic == 0), stop=(ic == 31)), inc=(ic == 31))
            _w("scalar", "sPE", g_pe)
            ge = act(lambda e: e.activation(sGeb[:], pSmall[0:4, :], AF.Exp), inc=True)
            _w("tensor", "sACT", ge)
            gs_pe = pe(lambda e: e.matmul(pSmall[0:4, :], sOnes[0:4, 0:4], sGeb[:], start=True, stop=True), inc=True)
            _w("scalar", "sPE", gs_pe)
            lng = act(lambda e: e.activation(sGl[:], pSmall[0:4, :], AF.Ln), inc=True)
            _w("scalar", "sACT", lng)
            r4 = act(lambda e: e.activation(sR4[:], sGl[:], AF.Exp, scale=-1.0), inc=True)
            _w("vector", "sACT", r4)
            gn = dve(lambda e: e.tensor_tensor(sGn[:], sGeb[:], sR4[:], ALU.mult), inc=True)
            _w("tensor", "sDVE", gn)
            _w("tensor", "qE8", 16)
            grep_pe = pe(lambda e: e.matmul(pSmall[:], sE8[:], sGn[:], start=True, stop=True), inc=True)
            _w("vector", "sPE", max(pr_pe, grep_pe))
            pr_ev = dve(lambda e: e.tensor_copy(sPr32[:], pSmall2[:]), inc=True)
            pSm2_user = pr_ev
            prg = dve(lambda e: e.tensor_tensor(sPrg[:], pSmall[:], sPr32[:], ALU.mult), inc=True)
            prg_prev = prg

            evac5 = {}
            for oslab in range(8):
                for ic in range(32):
                    k, kb, cv = stripe("t", ic, oslab)
                    _w("tensor", *cv)
                    if ic == 0:
                        _w("tensor", "sDVE", prg)
                    for tt in range(4):
                        if ic == 0:
                            _w("tensor", "sDVE", pAo_user[tt])
                        val = pe(lambda e, ic=ic, tt=tt, kb=kb, st=(ic == 0):
                                 e.matmul(pAo[tt][:], sH1T[:, ic * 512 + tt * 128: ic * 512 + (tt + 1) * 128],
                                          sWb[kb][:], start=st, stop=False), inc=(tt == 3))
                    stripe_user[k] = val
                lora_done = {}
                for tt in range(4):
                    lora_done[tt] = pe(lambda e, tt=tt, oslab=oslab:
                                       e.matmul(pAo[tt][:], sPrg[:, tt * 128:(tt + 1) * 128],
                                                sBm[:, 512 * oslab:512 * (oslab + 1)],
                                                start=False, stop=True), inc=True)
                stripe_user[kglob[0] - 1] = lora_done[3]
                for tt in range(4):
                    _w("vector", "sPE", lora_done[tt])
                    ev = dve(lambda e, tt=tt, oslab=oslab: e.tensor_tensor(
                        sAct[:, tt * D + oslab * 512: tt * D + (oslab + 1) * 512],
                        pAo[tt][:], sAct[:, tt * D + oslab * 512: tt * D + (oslab + 1) * 512], ALU.add), inc=True)
                    pAo_user[tt] = ev
                    evac5[(oslab, tt)] = ev

            for tt in range(4):
                ap, apb = emit_ln(tt, evac5[(7, tt)])
                _w("scalar", "sACT", ap)
                store(lambda e, sl=sl, tt=tt: e.dma_start(
                    out=Y[512 * sl + 128 * tt: 512 * sl + 128 * (tt + 1), :],
                    in_=sAct[:, tt * D:(tt + 1) * D]), "qY")
            y_store_total = n["qY"]

    wait_all_stores("scalar")

    with contextlib.ExitStack() as emit_stack:
        sems = {}
        for name in ALL_SEMS:
            sems[name] = emit_stack.enter_context(nc.semaphore(name))
        block = emit_stack.enter_context(nc.Block())

        def mk(e):
            def emitfn(eng):
                for op in OPS[e]:
                    if op[0] == "w":
                        eng.wait_ge(sems[op[1]], op[2])
                    else:
                        ins = op[1](eng)
                        if op[2] is not None:
                            ins.then_inc(sems[op[2]], op[3])
            return emitfn

        block.sync(mk("sync"))
        block.tensor(mk("tensor"))
        block.vector(mk("vector"))
        block.scalar(mk("scalar"))
        block.gpsimd(mk("gpsimd"))

    es.close()
    return nc


def _prep_inputs(inputs):
    x = np.asarray(inputs["L0_attn_hidden_states"])[0]
    wq = np.asarray(inputs["L0_attn_wq"])
    wk = np.asarray(inputs["L0_attn_wk"])
    wv = np.asarray(inputs["L0_attn_wv"])
    wo = np.asarray(inputs["L0_attn_wo"])
    W = np.asarray(inputs["L0_ffn_W"])
    b = np.asarray(inputs["L0_ffn_b"])
    A = np.asarray(inputs["L0_ffn_A"])
    Bm = np.asarray(inputs["L0_ffn_Bm"])
    Wr = np.asarray(inputs["L0_ffn_Wr"])

    xT = np.ascontiguousarray(x.T)
    wt = np.ascontiguousarray(W.T)
    brep = np.ascontiguousarray(np.tile(b[None, :].astype(np.float32), (128, 1)))

    Apad = np.zeros((128, D), np.float32)
    for e in range(4):
        Apad[32 * e:32 * e + 8] = A[e]
    atm = np.ascontiguousarray(Apad.T.reshape(32, 128, 128).transpose(1, 0, 2).reshape(128, 32 * 128))
    bmt = np.zeros((128, D), np.float32)
    for e in range(4):
        bmt[32 * e:32 * e + 8] = Bm[e].T
    wrt = np.ascontiguousarray(Wr.T.reshape(32, 128, 4).transpose(1, 0, 2).reshape(128, 128))

    inv = 1.0 / 10000.0 ** (np.arange(0, HD, 2, dtype=np.float64) / HD)
    pos = np.arange(S, dtype=np.float64)
    fr = pos[:, None] * inv[None, :]
    emb = np.concatenate([fr, fr], axis=1)
    cosT = np.cos(emb).T.astype(np.float32)
    sinT = np.sin(emb).T.astype(np.float32)
    cs2 = np.empty((128, 2 * S), np.float32)
    for ts in range(NSLAB):
        cs2[:, 1024 * ts:1024 * ts + 512] = cosT[:, 512 * ts:512 * (ts + 1)]
        cs2[:, 1024 * ts + 512:1024 * (ts + 1)] = sinT[:, 512 * ts:512 * (ts + 1)]

    msk = np.zeros((128, 4 * QP), BF)
    ff = np.arange(QP)
    pp = np.arange(128)
    for dv in range(4):
        msk[:, dv * QP:(dv + 1) * QP] = (ff[None, :] >= (128 * dv + pp[:, None])).astype(BF)

    onesb = np.ones((128, 128), BF)
    ident = np.eye(128, dtype=np.float32).astype(BF)
    e8p = np.zeros((4, 128), BF)
    for e in range(4):
        e8p[e, 32 * e:32 * e + 8] = BF(2.0)
    cv = np.array([CEXP, SCL, -1.0, -0.5, EPS, 1.0, 1.0 / D, -1.0 / D, 0.0, 2.0, 0.5, -2.0],
                  np.float32)
    consts = np.tile(np.concatenate([cv, np.zeros(4, np.float32)]), (128, 1))

    def wslice(w, c):
        ws = w[:, FPC * c:FPC * (c + 1)]
        return np.ascontiguousarray(ws.reshape(32, 128, FPC).transpose(1, 0, 2).reshape(128, 32 * FPC))

    in_maps = []
    for c in range(NCORE):
        in_maps.append({
            "xt": xT,
            "xr": np.ascontiguousarray(x[TPC * c:TPC * (c + 1), :]),
            "wq": wslice(wq, c),
            "wk": wslice(wk, c),
            "wv": wslice(wv, c),
            "wo": wo,
            "wt": wt,
            "brep": brep,
            "atm": atm,
            "bmt": bmt,
            "wrt": wrt,
            "cs2": cs2,
            "msk": msk,
            "onesb": onesb,
            "ident": ident,
            "e8p": e8p,
            "consts": consts,
        })
    return in_maps


def kernel(**inputs):
    nc = build_program()
    in_maps = _prep_inputs(inputs)
    res = run_bass_kernel_spmd(nc, in_maps, core_ids=list(range(NCORE)))
    out = np.concatenate([np.asarray(res.results[c]["y"]) for c in range(NCORE)], axis=0)
    return out.reshape(1, S, D).astype(np.float32)


# version 15 (final)
# speedup vs baseline: 1.0482x; 1.0227x over previous
import os
import math
import contextlib

os.environ.setdefault("TRN_ALLOW_ICI_ALL_TO_ALL", "1")

import numpy as np
import ml_dtypes

import concourse.bass as bass
import concourse.mybir as mybir
from concourse.bass_utils import run_bass_kernel_spmd
from concourse.aot_env import allow_ici_all_to_all

F32 = mybir.dt.float32
BF16 = mybir.dt.bfloat16
AF = mybir.ActivationFunctionType
ALU = mybir.AluOpType
BF = ml_dtypes.bfloat16

NCORE = 8
S = 8192
D = 4096
HD = 128
HPC = 4
FPC = HPC * HD
TPC = S // NCORE
SLAB = 512
NSLAB = S // SLAB
QP = 512
NPAN = S // QP
SCL = 1.0 / math.sqrt(HD)
CEXP = -10.0
EPS = 1e-5
NXB = 4

ENGINES = ("sync", "tensor", "vector", "scalar", "gpsimd")
DMA_SEMS = (["qC", "qCN", "qON", "qID", "qE8", "qM", "qB", "qWQ", "qWK", "qWV", "qCSa", "qCSb", "qKV", "qKVb", "qQa", "qQb", "qOG", "qOGB", "qXS", "qY", "qPCa", "qPCb"]
            + [f"qX{i}" for i in range(NXB)]
            + ["qR0", "qR1"] + [f"qSV{i}" for i in range(4)]
            + ["qO0", "qO1"] + [f"qW{i}" for i in range(8)])
ALL_SEMS = DMA_SEMS + ["sPE", "sDVE", "sACT", "sPOOL", "sCCa", "sCCb", "sBAR"]


def build_program():
    nc = bass.Bass("TRN2", target_bir_lowering=False, debug=False)

    XT = nc.dram_tensor("xt", [D, S], F32, kind="ExternalInput").ap()
    XR = nc.dram_tensor("xr", [TPC, D], F32, kind="ExternalInput").ap()
    WQ = nc.dram_tensor("wq", [128, 32 * FPC], F32, kind="ExternalInput").ap()
    WK = nc.dram_tensor("wk", [128, 32 * FPC], F32, kind="ExternalInput").ap()
    WV = nc.dram_tensor("wv", [128, 32 * FPC], F32, kind="ExternalInput").ap()
    WO = nc.dram_tensor("wo", [D, D], F32, kind="ExternalInput").ap()
    WT = nc.dram_tensor("wt", [D, D], F32, kind="ExternalInput").ap()
    BREP = nc.dram_tensor("brep", [128, D], F32, kind="ExternalInput").ap()
    ATM = nc.dram_tensor("atm", [128, 32 * 128], F32, kind="ExternalInput").ap()
    BMT = nc.dram_tensor("bmt", [128, D], F32, kind="ExternalInput").ap()
    WRT = nc.dram_tensor("wrt", [128, 128], F32, kind="ExternalInput").ap()
    CS2 = nc.dram_tensor("cs2", [128, 2 * S], F32, kind="ExternalInput").ap()
    MSK = nc.dram_tensor("msk", [128, 4 * QP], BF16, kind="ExternalInput").ap()
    ONESB = nc.dram_tensor("onesb", [128, 128], BF16, kind="ExternalInput").ap()
    IDENT = nc.dram_tensor("ident", [128, 128], BF16, kind="ExternalInput").ap()
    E8P = nc.dram_tensor("e8p", [4, 128], BF16, kind="ExternalInput").ap()
    CONSTS = nc.dram_tensor("consts", [128, 16], F32, kind="ExternalInput").ap()
    Y = nc.dram_tensor("y", [TPC, D], F32, kind="ExternalOutput").ap()

    QTd = nc.dram_tensor("qtd", [FPC, S], BF16)
    KTd = nc.dram_tensor("ktd", [FPC, S], BF16)
    Vd = nc.dram_tensor("vd", [S, FPC], BF16)
    CCIA = nc.dram_tensor("ccia", [D // 2, TPC], BF16)
    CCIB = nc.dram_tensor("ccib", [D // 2, TPC], BF16)
    CCOA = nc.dram_tensor("ccoa", [D // 2, TPC], BF16)
    CCOB = nc.dram_tensor("ccob", [D // 2, TPC], BF16)
    WOB = nc.dram_tensor("wob", [8 * 32 * 128, 512], BF16)
    WTB = nc.dram_tensor("wtb", [8 * 32 * 128, 512], BF16)

    OPS = {e: [] for e in ENGINES}
    n = {"PE": 0, "DVE": 0, "ACT": 0, "POOL": 0, "BAR": 0}
    for s_ in DMA_SEMS:
        n[s_] = 0
    hwm = {e: {} for e in ENGINES}

    def _w(e, sem, val):
        if val is None or val <= 0:
            return
        if hwm[e].get(sem, 0) >= val:
            return
        hwm[e][sem] = val
        OPS[e].append(("w", sem, val))

    def _i(e, fn, sem=None, inc=1):
        OPS[e].append(("i", fn, sem, inc))

    def load(fn, sem):
        n[sem] += 16
        _i("sync", fn, sem, 16)
        return (sem, n[sem])

    def gload(fn, sem):
        n[sem] += 16
        _i("gpsimd", fn, sem, 16)
        return (sem, n[sem])

    def loadq(engq, fn, sem):
        n[sem] += 16
        _i(engq, fn, sem, 16)
        return (sem, n[sem])

    def store(fn, sem):
        n[sem] += 16
        _i("scalar", fn, sem, 16)
        return (sem, n[sem])

    def pe(fn, inc=False):
        if inc:
            n["PE"] += 1
            _i("tensor", fn, "sPE", 1)
            return n["PE"]
        _i("tensor", fn)
        return None

    def dve(fn, inc=True):
        _w("vector", "sDVE", n["DVE"])
        n["DVE"] += 1
        _i("vector", fn, "sDVE", 1)
        return n["DVE"]

    def act(fn, inc=True):
        _w("scalar", "sACT", n["ACT"])
        n["ACT"] += 1
        _i("scalar", fn, "sACT", 1)
        return n["ACT"]

    def pool(fn, inc=True):
        _w("gpsimd", "sPOOL", n["POOL"])
        n["POOL"] += 1
        _i("gpsimd", fn, "sPOOL", 1)
        return n["POOL"]

    def barrier():
        n["BAR"] += 1
        tgt = 5 * n["BAR"]
        for e in ENGINES:
            _i(e, lambda eng: eng.nop(), "sBAR", 1)
        for e in ENGINES:
            hwm[e]["sBAR"] = tgt
            OPS[e].append(("w", "sBAR", tgt))

    def wait_all_stores(e):
        for s_ in ("qR0", "qR1", "qSV0", "qSV1", "qSV2", "qSV3", "qO0", "qO1", "qY"):
            _w(e, s_, n[s_])

    es = contextlib.ExitStack()

    sconst = es.enter_context(nc.sbuf_tensor("sconst", [128, 16], F32))
    sOnes = es.enter_context(nc.sbuf_tensor("sOnes", [128, 128], BF16))
    sIdent = es.enter_context(nc.sbuf_tensor("sIdent", [128, 128], BF16))
    sE8 = es.enter_context(nc.sbuf_tensor("sE8", [4, 128], BF16))
    sAtb = es.enter_context(nc.sbuf_tensor("sAtb", [128, 32 * 128], BF16))
    sWrTb = es.enter_context(nc.sbuf_tensor("sWrTb", [128, 128], BF16))
    sBm = es.enter_context(nc.sbuf_tensor("sBm", [128, D], BF16))
    sBrep = es.enter_context(nc.sbuf_tensor("sBrep", [128, D], F32))
    sWb = [es.enter_context(nc.sbuf_tensor(f"sWb{i}", [128, 512], BF16)) for i in range(8)]
    sOg = es.enter_context(nc.sbuf_tensor("sOg", [128, 32 * 512], BF16))

    stripe_user = {}
    kglob = [0]

    def stripe(which, ic, oslab):
        k = kglob[0]
        kb = k % 8
        Wap = WO if which == "o" else WT
        _w("gpsimd", "sPE", stripe_user.get(k - 8, 0))
        lw = gload(lambda e, ic=ic, oslab=oslab, kb=kb, Wap=Wap: e.dma_start(
            out=sWb[kb][:], in_=Wap[128 * ic:128 * (ic + 1), 512 * oslab:512 * (oslab + 1)]), f"qW{kb}")
        kglob[0] += 1
        return k, kb, lw

    CONST_VALS = [CEXP, SCL, -1.0, -0.5, EPS, 1.0, 1.0 / D, -1.0 / D, 0.0, 2.0, 0.5, -2.0]
    for j, v in enumerate(CONST_VALS):
        nc.const_aps.register(F32, float(v), sconst[:, j:j + 1])

    with contextlib.ExitStack() as p1:
        swq = p1.enter_context(nc.sbuf_tensor("swq", [128, 32 * FPC], BF16))
        swk = p1.enter_context(nc.sbuf_tensor("swk", [128, 32 * FPC], BF16))
        sxb = [p1.enter_context(nc.sbuf_tensor(f"sxb{i}", [128, 4 * SLAB], BF16)) for i in range(NXB)]
        scs = [p1.enter_context(nc.sbuf_tensor(f"scs{i}", [128, 2 * SLAB], F32)) for i in range(2)]
        sQf = [p1.enter_context(nc.sbuf_tensor(f"sQf{i}", [128, SLAB], F32)) for i in range(8)]
        sRot = [p1.enter_context(nc.sbuf_tensor(f"sRot{i}", [128, SLAB], F32)) for i in range(4)]
        sQc = [p1.enter_context(nc.sbuf_tensor(f"sQc{i}", [128, SLAB], F32)) for i in range(2)]
        sQs = [p1.enter_context(nc.sbuf_tensor(f"sQs{i}", [128, SLAB], F32)) for i in range(2)]
        sQr = [p1.enter_context(nc.sbuf_tensor(f"sQr{i}", [128, SLAB], BF16)) for i in range(2)]
        pQK = [p1.enter_context(nc.psum_tensor(f"pQK{j}", [128, SLAB], F32)) for j in range(8)]

        load(lambda e: e.dma_start(out=sconst[:], in_=CONSTS), "qCN")
        load(lambda e: e.dma_start(out=sOnes[:], in_=ONESB), "qON")
        load(lambda e: e.dma_start(out=sIdent[:], in_=IDENT), "qID")
        load(lambda e: e.dma_start(out=sE8[:], in_=E8P), "qE8")
        load(lambda e: e.dma_start(out=sBrep[:], in_=BREP), "qB")
        gload(lambda e: e.dma_start(out=sAtb[:], in_=ATM), "qC")
        gload(lambda e: e.dma_start(out=sWrTb[:], in_=WRT), "qC")
        gload(lambda e: e.dma_start(out=sBm[:], in_=BMT), "qC")
        p4c_total = n["qC"]
        wq_l = gload(lambda e: e.dma_start(out=swq[:], in_=WQ), "qWQ")
        wk_l = gload(lambda e: e.dma_start(out=swk[:], in_=WK), "qWK")

        xload = {}
        xconv = {}
        icgrp = {}
        bankfree = {}
        rotdone = {}
        st_qk = {}
        cs_load = {}

        def p1a_finish_slab(ts):
            cb = ts % 2
            for j in range(8):
                _w("scalar", "sPE", icgrp[ts * 8 + 7])
                _w("scalar", "sDVE", rotdone.get((ts - 1, j), 0))
                bankfree[(ts, j)] = act(lambda e, j=j: e.copy(sQf[j][:], pQK[j][:]))
                _w("vector", "sACT", bankfree[(ts, j)])
                pj4 = (ts, j - 4) if j >= 4 else (ts - 1, j + 4)
                _w("vector", "sDVE", rotdone.get(pj4, 0))
                dve(lambda e, j=j, b=j % 4: e.tensor_scalar_mul(sRot[b][0:64, :], sQf[j][64:128, :], -1.0))
                rp = dve(lambda e, j=j, b=j % 4: e.tensor_copy(sRot[b][64:128, :], sQf[j][0:64, :]))
                _w("vector", *cs_load[ts])
                pj = (ts, j - 2) if j >= 2 else (ts - 1, j + 6)
                if pj in st_qk:
                    _w("vector", *st_qk[pj])
                dve(lambda e, j=j, b=j % 2, cb=cb: e.tensor_tensor(sQc[b][:], sQf[j][:], scs[cb][:, 0:SLAB], ALU.mult))
                dve(lambda e, b=j % 4, bb=j % 2, cb=cb: e.tensor_tensor(sQs[bb][:], sRot[b][:], scs[cb][:, SLAB:2 * SLAB], ALU.mult))
                rotdone[(ts, j)] = dve(lambda e, b=j % 2: e.tensor_tensor(sQr[b][:], sQc[b][:], sQs[b][:], ALU.add))
                _w("scalar", "sDVE", rotdone[(ts, j)])
                dst = QTd if j < 4 else KTd
                oc = j % 4
                st_qk[(ts, j)] = store(lambda e, dst=dst, oc=oc, tp=SLAB * ts, b=j % 2: e.dma_start(
                    out=dst.ap()[128 * oc:128 * (oc + 1), tp:tp + SLAB], in_=sQr[b][:]),
                    "qR0" if j % 2 == 0 else "qR1")

        for ts in range(NSLAB):
            t0 = SLAB * ts
            _w("sync", "sDVE", rotdone.get((ts - 2, 7), 0))
            cs_load[ts] = load(lambda e, ts=ts, b=ts % 2: e.dma_start(
                out=scs[b][:], in_=CS2[:, 1024 * ts:1024 * (ts + 1)]),
                "qCSa" if ts % 2 == 0 else "qCSb")
            def p1a_loadconv(gq):
                g = ts * 8 + gq
                _w("gpsimd", "sPE", icgrp.get(g - NXB, 0))
                xload[g] = gload(lambda e, t0=t0, gq=gq, b=g % NXB: e.dma_start(
                    out=sxb[b][:].rearrange("p (c t) -> p c t", c=4),
                    in_=XT.rearrange("(c p) t -> p c t", p=128)[:, 4 * gq:4 * (gq + 1), t0:t0 + SLAB]),
                    f"qX{g % NXB}")
                xconv[g] = xload[g]

            for gq in range(2):
                p1a_loadconv(gq)
            if ts >= 1:
                p1a_finish_slab(ts - 1)
            if ts == 0:
                _w("tensor", *wq_l)
                _w("tensor", *wk_l)
            for ic in range(32):
                g = ts * 8 + ic // 4
                if ic >= 8 and ic % 4 == 0:
                    p1a_loadconv(ic // 4)
                _w("tensor", *xconv[g])
                for j in range(8):
                    if ic == 0:
                        _w("tensor", "sACT", bankfree.get((ts - 1, j), 0))
                    src = swq if j < 4 else swk
                    oc = j % 4
                    val = pe(lambda e, j=j, ic=ic, src=src, oc=oc, b=g % NXB, st=(ic == 0), sp=(ic == 31):
                             e.matmul(pQK[j][:], src[:, ic * FPC + oc * 128: ic * FPC + (oc + 1) * 128],
                                      sxb[b][:, (ic % 4) * SLAB:(ic % 4 + 1) * SLAB], start=st, stop=sp),
                             inc=(j == 7))
                    if j == 7:
                        icgrp[ts * 8 + ic // 4] = val

        p1a_finish_slab(NSLAB - 1)

    barrier()

    with contextlib.ExitStack() as p1b:
        swv = p1b.enter_context(nc.sbuf_tensor("swv", [128, 32 * FPC], BF16))
        sxbB = [p1b.enter_context(nc.sbuf_tensor(f"sxbB{i}", [128, 4 * SLAB], BF16)) for i in range(NXB)]
        sVe = [p1b.enter_context(nc.sbuf_tensor(f"sVe{i}", [128, FPC], BF16)) for i in range(4)]
        pV = [p1b.enter_context(nc.psum_tensor(f"pV{j}", [128, FPC], F32)) for j in range(8)]

        wv_l = gload(lambda e: e.dma_start(out=swv[:], in_=WV), "qWV")

        xloadB, xconvB, icgrpB, bfreeB, stV = {}, {}, {}, {}, {}

        def p1b_evacs(ts):
            half = (ts % 2) * 4
            for tt in range(4):
                _w("scalar", "sPE", icgrpB[ts * 8 + 7])
                if (ts - 1, tt) in stV:
                    _w("scalar", *stV[(ts - 1, tt)])
                bfreeB[(ts, tt)] = act(lambda e, tt=tt, half=half: e.copy(sVe[tt][:], pV[half + tt][:]))
                _w("scalar", "sACT", bfreeB[(ts, tt)])
                stV[(ts, tt)] = store(lambda e, tp=SLAB * ts, tt=tt: e.dma_start(
                    out=Vd.ap()[tp + 128 * tt: tp + 128 * (tt + 1), :], in_=sVe[tt][:]), f"qSV{tt}")

        for ts in range(NSLAB):
            t0 = SLAB * ts
            def p1b_loadconv(gq):
                g = ts * 8 + gq
                _w("gpsimd", "sPE", icgrpB.get(g - NXB, 0))
                xloadB[g] = gload(lambda e, t0=t0, gq=gq, b=g % NXB: e.dma_start(
                    out=sxbB[b][:].rearrange("p (c t) -> p c t", c=4),
                    in_=XT.rearrange("(c p) t -> p c t", p=128)[:, 4 * gq:4 * (gq + 1), t0:t0 + SLAB]),
                    f"qX{g % NXB}")
                xconvB[g] = xloadB[g]

            for gq in range(2):
                p1b_loadconv(gq)
            if ts >= 1:
                p1b_evacs(ts - 1)
            if ts == 0:
                _w("tensor", *wv_l)
            half = (ts % 2) * 4
            for ic in range(32):
                g = ts * 8 + ic // 4
                if ic >= 8 and ic % 4 == 0:
                    p1b_loadconv(ic // 4)
                _w("tensor", *xconvB[g])
                for tt in range(4):
                    if ic == 0:
                        _w("tensor", "sACT", bfreeB.get((ts - 2, tt), 0))
                    val = pe(lambda e, tt=tt, ic=ic, b=g % NXB, st=(ic == 0), sp=(ic == 31), half=half:
                             e.matmul(pV[half + tt][:], sxbB[b][:, (ic % 4) * SLAB + tt * 128:(ic % 4) * SLAB + (tt + 1) * 128],
                                      swv[:, ic * FPC:(ic + 1) * FPC], start=st, stop=sp),
                             inc=(tt == 3))
                    if tt == 3:
                        icgrpB[g] = val
        p1b_evacs(NSLAB - 1)

    barrier()

    with contextlib.ExitStack() as p2:
        sMask = p2.enter_context(nc.sbuf_tensor("sMask", [128, 4 * QP], BF16))
        skT = [p2.enter_context(nc.sbuf_tensor(f"skT{i}", [128, S], BF16)) for i in range(2)]
        sv = [p2.enter_context(nc.sbuf_tensor(f"sv{i}", [128, S], BF16)) for i in range(2)]
        sqTp = [p2.enter_context(nc.sbuf_tensor(f"sqTp{i}", [128, QP], BF16)) for i in range(2)]
        sP = [p2.enter_context(nc.sbuf_tensor(f"sP{i}", [128, QP], BF16)) for i in range(4)]
        sPt = [p2.enter_context(nc.sbuf_tensor(f"sPt{i}", [128, QP], BF16)) for i in range(2)]
        sRec = [p2.enter_context(nc.sbuf_tensor(f"sRec{i}", [128, QP], F32)) for i in range(2)]
        sLn2b = [p2.enter_context(nc.sbuf_tensor(f"sLn2b{i}", [128, QP], F32)) for i in range(2)]
        sOn = [p2.enter_context(nc.sbuf_tensor(f"sOn{i}", [128, QP], BF16)) for i in range(2)]
        pS = [p2.enter_context(nc.psum_tensor(f"pS{i}", [128, QP], F32)) for i in range(3)]
        pO = [p2.enter_context(nc.psum_tensor(f"pO{i}", [128, QP], F32)) for i in range(2)]
        pSm = [p2.enter_context(nc.psum_tensor(f"pSm{i}", [128, QP], F32)) for i in range(2)]

        for s_ in ("qR0", "qR1", "qSV0", "qSV1", "qSV2", "qSV3"):
            _w("sync", s_, n[s_])
        lmask = load(lambda e: e.dma_start(out=sMask[:], in_=MSK), "qM")
        _w("scalar", "qCN", 16)
        _w("vector", *lmask)
        _w("tensor", "qON", 16)

        qload = {}
        kv_done = {}
        qbuf_user = {}
        pS_user = {}
        sP_user = {}
        sPt_user = {}
        pO_user = {}
        pSm_user = {}
        sOn_user = {}
        head_pe_end = 0

        head_ends = {}

        def load_head_kv(h):
            hb = h % 2
            sem = "qKV" if hb == 0 else "qKVb"
            load(lambda e, h=h, hb=hb: e.dma_start(out=skT[hb][:], in_=KTd.ap()[128 * h:128 * (h + 1), :]), sem)
            for g in range(8):
                load(lambda e, h=h, g=g, hb=hb: e.dma_start(
                    out=sv[hb][:, g * 1024:(g + 1) * 1024].rearrange("p (kt d) -> p kt d", d=128),
                    in_=Vd.ap().rearrange("(kt p) (hh d) -> p kt hh d", p=128, d=128)[:, 8 * g:8 * (g + 1), h, :]), sem)
            kv_done[h] = (sem, n[sem])

        load_head_kv(0)
        qo_after_h1 = None
        for h in range(HPC):
            hb = h % 2
            if h + 1 < HPC:
                _w("sync", "sPE", head_ends.get(h - 1, 0))
                load_head_kv(h + 1)

            for p in range(NPAN):
                nkt = 4 * (p + 1)
                qb = p % 2
                ob = p % 2
                _w("sync", "sPE", qbuf_user.get(qb, 0))
                qload[(h, p)] = load(lambda e, h=h, p=p, qb=qb: e.dma_start(
                    out=sqTp[qb][:], in_=QTd.ap()[128 * h:128 * (h + 1), QP * p:QP * (p + 1)]),
                    "qQa" if qb == 0 else "qQb")

                exp_done = {}
                last_sum = 0

                def emit_score(kt):
                    sb = kt % 3
                    if kt == 0:
                        _w("tensor", *qload[(h, p)])
                        _w("tensor", *kv_done[h])
                    _w("tensor", "sACT", pS_user.get(sb, 0))
                    v = pe(lambda e, kt=kt, sb=sb, qb=qb, hb=hb: e.matmul(
                        pS[sb][:], skT[hb][:, kt * 128:(kt + 1) * 128], sqTp[qb][:], start=True, stop=True), inc=True)
                    qbuf_user[qb] = v
                    return v

                peS = {0: emit_score(0)}
                if nkt > 1:
                    peS[1] = emit_score(1)
                for kt in range(nkt):
                    sb = kt % 3
                    pb = kt % 4
                    diag = (kt >= 4 * p)
                    _w("scalar", "sPE", peS[kt])
                    if diag:
                        _w("scalar", "sDVE", sPt_user.get(kt % 2, 0))
                        tgt = sPt[kt % 2]
                    else:
                        _w("scalar", "sPE", sP_user.get(pb, 0))
                        tgt = sP[pb]
                    exp_done[kt] = act(lambda e, sb=sb, tgt=tgt: e.activation(
                        tgt[:], pS[sb][:], AF.Exp, bias=CEXP, scale=SCL), inc=True)
                    pS_user[sb] = exp_done[kt]
                    if diag:
                        dv = kt - 4 * p
                        _w("vector", "sACT", exp_done[kt])
                        _w("vector", "sPE", sP_user.get(pb, 0))
                        mk_ = dve(lambda e, pb=pb, tb=kt % 2, dv=dv: e.tensor_tensor(
                            sP[pb][:], sPt[tb][:], sMask[:, dv * QP:(dv + 1) * QP], ALU.mult))
                        sPt_user[kt % 2] = mk_
                    if kt + 2 < nkt:
                        peS[kt + 2] = emit_score(kt + 2)
                    if diag:
                        _w("tensor", "sDVE", mk_)
                    else:
                        _w("tensor", "sACT", exp_done[kt])
                    if kt == 0:
                        _w("tensor", "sDVE", pO_user.get(ob, 0))
                        _w("tensor", "sACT", pSm_user.get(ob, 0))
                    pe(lambda e, kt=kt, pb=pb, ob=ob, sp=(kt == nkt - 1), hb=hb: e.matmul(
                        pO[ob][:], sv[hb][:, kt * 128:(kt + 1) * 128], sP[pb][:],
                        start=(kt == 0), stop=sp))
                    last_sum = pe(lambda e, kt=kt, pb=pb, ob=ob, sp=(kt == nkt - 1): e.matmul(
                        pSm[ob][:], sOnes[:], sP[pb][:],
                        start=(kt == 0), stop=sp), inc=True)
                    sP_user[pb] = last_sum
                head_pe_end = last_sum
                head_ends[h] = last_sum
                _w("scalar", "sPE", last_sum)
                _w("scalar", "sACT", pSm_user.get(ob, 0))
                lnc = act(lambda e, ob=ob: e.activation(sLn2b[ob][:], pSm[ob][:], AF.Ln), inc=True)
                _w("scalar", "sACT", lnc)
                epi = act(lambda e, ob=ob: e.activation(sRec[ob][:], sLn2b[ob][:], AF.Exp, scale=-1.0), inc=True)
                pSm_user[ob] = epi
                _w("vector", "sACT", epi)
                if ob in sOn_user:
                    _w("vector", *sOn_user[ob])
                nrm = dve(lambda e, ob=ob: e.tensor_tensor(sOn[ob][:], pO[ob][:], sRec[ob][:], ALU.mult), inc=True)
                pO_user[ob] = nrm
                _w("scalar", "sDVE", nrm)
                cct = CCIA if h < 2 else CCIB
                r0 = 256 * (p // 2) + 128 * (h % 2)
                c0 = 512 * (p % 2)
                sOn_user[ob] = store(lambda e, cct=cct, r0=r0, c0=c0, ob=ob: e.dma_start(
                    out=cct.ap()[r0:r0 + 128, c0:c0 + QP], in_=sOn[ob][:]), "qO0" if ob == 0 else "qO1")
            if h == 1:
                qo_after_h1 = (n["qO0"], n["qO1"])

    _w("gpsimd", "qO0", qo_after_h1[0])
    _w("gpsimd", "qO1", qo_after_h1[1])
    with allow_ici_all_to_all():
        _i("gpsimd", lambda e: e.collective_compute(
            "AllToAll", ALU.bypass, replica_groups=[list(range(NCORE))],
            ins=[CCIA.ap().opt()], outs=[CCOA.ap().opt()]), "sCCa", 1)
    wait_all_stores("gpsimd")
    with allow_ici_all_to_all():
        _i("gpsimd", lambda e: e.collective_compute(
            "AllToAll", ALU.bypass, replica_groups=[list(range(NCORE))],
            ins=[CCIB.ap().opt()], outs=[CCOB.ap().opt()]), "sCCb", 1)

    _w("sync", "sCCa", 1)
    for hh in range(2):
        loadq("sync", lambda e, hh=hh: e.dma_start(
            out=sOg[:].rearrange("p (r q) -> p r q", q=2048)[:, :, 512 * hh:512 * (hh + 1)],
            in_=CCOA.ap().rearrange("(r hh p) t -> p r hh t", p=128, hh=2)[:, :, hh, 0:512]), "qOG")
    ogA_pre = n["qOG"]
    IC_ORDER = [ic for ic in range(32) if ic % 4 < 2] + [ic for ic in range(32) if ic % 4 >= 2]
    pre8 = [stripe("o", IC_ORDER[i], 0) for i in range(8)]

    barrier()

    with contextlib.ExitStack() as p4:
        sAct = p4.enter_context(nc.sbuf_tensor("sAct", [128, 4 * D], F32))
        sH1bt = p4.enter_context(nc.sbuf_tensor("sH1bt", [128, D], BF16))
        sH1T = p4.enter_context(nc.sbuf_tensor("sH1T", [128, 32 * 512], BF16))
        sXs = [p4.enter_context(nc.sbuf_tensor(f"sXs{i}", [128, 4 * 512], F32)) for i in range(1)]
        sStat = p4.enter_context(nc.sbuf_tensor("sStat", [128, 8], F32))
        sGeb = p4.enter_context(nc.sbuf_tensor("sGeb", [4, 512], BF16))
        sGl = p4.enter_context(nc.sbuf_tensor("sGl", [4, 512], BF16))
        sR4 = p4.enter_context(nc.sbuf_tensor("sR4", [4, 512], BF16))
        sGn = p4.enter_context(nc.sbuf_tensor("sGn", [4, 512], BF16))
        sPr32 = p4.enter_context(nc.sbuf_tensor("sPr32", [128, 512], F32))
        sPrg = p4.enter_context(nc.sbuf_tensor("sPrg", [128, 512], BF16))
        pAo = [p4.enter_context(nc.psum_tensor(f"pAo{i}", [128, 512], F32)) for i in range(4)]
        pTr = [p4.enter_context(nc.psum_tensor(f"pTr{i}", [128, 1024], BF16)) for i in range(2)]
        pSmall = p4.enter_context(nc.psum_tensor("pSmall", [128, 512], F32))
        pSmall2 = p4.enter_context(nc.psum_tensor("pSmall2", [128, 512], F32))

        pAo_user = {tt: 0 for tt in range(4)}
        sXs_user = 0
        pTr_user = {0: 0, 1: 0}
        y_store_total = 0
        outproj_pe_end = 0
        prg_prev = 0
        pSm2_user = 0

        def emit_ln(tt, ready_dve, ready_pe=0):
            _w("scalar", "sDVE", ready_dve)
            _w("scalar", "sPE", ready_pe)
            act(lambda e, tt=tt: e.activation(sH1bt[:], sAct[:, tt * D:(tt + 1) * D],
                                              AF.Square, accum_out=sStat[:, 1:2]))
            sm = act(lambda e, tt=tt: e.activation(sH1bt[:], sAct[:, tt * D:(tt + 1) * D],
                                                   AF.Copy, accum_out=sStat[:, 0:1]))
            _w("vector", "sACT", sm)
            dve(lambda e: e.tensor_scalar_mul(sStat[:, 2:3], sStat[:, 0:1], 1.0 / D))
            m2 = dve(lambda e: e.tensor_scalar_mul(sStat[:, 3:4], sStat[:, 0:1], -1.0 / D))
            _w("vector", "sDVE", m2)
            m3 = dve(lambda e: e.tensor_tensor(sStat[:, 4:5], sStat[:, 2:3], sStat[:, 2:3], ALU.mult))
            m4 = dve(lambda e: e.tensor_scalar_mul(sStat[:, 5:6], sStat[:, 1:2], 1.0 / D))
            _w("vector", "sDVE", m4)
            st = dve(lambda e: e.tensor_tensor(sStat[:, 5:6], sStat[:, 5:6], sStat[:, 4:5], ALU.subtract))
            _w("scalar", "sDVE", st)
            lnc = act(lambda e: e.activation(sStat[:, 6:7], sStat[:, 5:6], AF.Ln, bias=EPS))
            _w("scalar", "sACT", lnc)
            rs = act(lambda e: e.activation(sStat[:, 7:8], sStat[:, 6:7], AF.Exp, scale=-0.5))
            _w("vector", "sACT", rs)
            nm = dve(lambda e: e.tensor_tensor(sStat[:, 4:5], sStat[:, 3:4], sStat[:, 7:8], ALU.mult))
            _w("scalar", "sDVE", nm)
            apb = act(lambda e, tt=tt: e.activation(sH1bt[:], sAct[:, tt * D:(tt + 1) * D],
                                                    AF.Identity, bias=sStat[:, 4:5], scale=sStat[:, 7:8]))
            ap = act(lambda e, tt=tt: e.activation(sAct[:, tt * D:(tt + 1) * D], sAct[:, tt * D:(tt + 1) * D],
                                                   AF.Identity, bias=sStat[:, 4:5], scale=sStat[:, 7:8]))
            return ap, apb

        for sl in range(2):
            if sl == 0:
                ogA_done = ogA_pre
            else:
                _w("sync", "sPE", outproj_pe_end)
                for hh in range(2):
                    load(lambda e, sl=sl, hh=hh: e.dma_start(
                        out=sOg[:].rearrange("p (r q) -> p r q", q=2048)[:, :, 512 * hh:512 * (hh + 1)],
                        in_=CCOA.ap().rearrange("(r hh p) t -> p r hh t", p=128, hh=2)[:, :, hh, 512 * sl:512 * (sl + 1)]), "qOG")
                ogA_done = n["qOG"]
            _w("scalar", "sCCb", 1)
            _w("scalar", "sPE", outproj_pe_end)
            for hh in range(2):
                loadq("scalar", lambda e, sl=sl, hh=hh: e.dma_start(
                    out=sOg[:].rearrange("p (r q) -> p r q", q=2048)[:, :, 1024 + 512 * hh:1024 + 512 * (hh + 1)],
                    in_=CCOB.ap().rearrange("(r hh p) t -> p r hh t", p=128, hh=2)[:, :, hh, 512 * sl:512 * (sl + 1)]), "qOGB")
            ogB_done = n["qOGB"]

            evac_dve = {}
            for oslab in range(8):
                for idx in range(32):
                    ic = IC_ORDER[idx]
                    if sl == 0 and oslab == 0 and idx < 8:
                        k, kb, cv = pre8[idx]
                    else:
                        k, kb, cv = stripe("o", ic, oslab)
                    _w("tensor", *cv)
                    if idx == 0:
                        _w("tensor", "qOG", ogA_done)
                        _w("tensor", "qC", p4c_total)
                    if idx == 16:
                        _w("tensor", "qOGB", ogB_done)
                    for tt in range(4):
                        if idx == 0:
                            _w("tensor", "sDVE", pAo_user[tt])
                        val = pe(lambda e, ic=ic, tt=tt, kb=kb, st=(idx == 0), sp=(idx == 31):
                                 e.matmul(pAo[tt][:], sOg[:, ic * 512 + tt * 128: ic * 512 + (tt + 1) * 128],
                                          sWb[kb][:], start=st, stop=sp), inc=(tt == 3))
                    stripe_user[k] = val
                outproj_pe_end = stripe_user[kglob[0] - 1]
                _w("sync", "sDVE", sXs_user)
                for tt in range(4):
                    load(lambda e, sl=sl, tt=tt, oslab=oslab: e.dma_start(
                        out=sXs[0][:, tt * 512:(tt + 1) * 512],
                        in_=XR[512 * sl + 128 * tt:512 * sl + 128 * (tt + 1), 512 * oslab:512 * (oslab + 1)]), "qXS")
                xs_done = n["qXS"]
                for tt in range(4):
                    _w("vector", "sPE", stripe_user[kglob[0] - 1])
                    _w("vector", "qXS", xs_done)
                    _w("vector", "qY", y_store_total)
                    ev = dve(lambda e, tt=tt, oslab=oslab: e.tensor_tensor(
                        sAct[:, tt * D + oslab * 512: tt * D + (oslab + 1) * 512],
                        pAo[tt][:], sXs[0][:, tt * 512:(tt + 1) * 512], ALU.add), inc=True)
                    pAo_user[tt] = ev
                    evac_dve[(oslab, tt)] = ev
                sXs_user = evac_dve[(oslab, 3)]

            h1t_done = 0
            h1t_last_pe = 0
            _w("tensor", "qID", 16)
            for tt in range(4):
                ap, apb = emit_ln(tt, evac_dve[(7, tt)], ready_pe=h1t_last_pe)
                _w("vector", "qB", 16)
                _w("vector", "sACT", ap)
                dve(lambda e, tt=tt: e.tensor_tensor(
                    sAct[:, tt * D:(tt + 1) * D], sAct[:, tt * D:(tt + 1) * D], sBrep[:], ALU.add))
                for ocb in range(8):
                    _w("tensor", "sACT", apb)
                    _w("tensor", "sDVE", pTr_user[ocb % 2])
                    last = None
                    for u in range(4):
                        oc = ocb * 4 + u
                        last = pe(lambda e, oc=oc, u=u, b=ocb % 2: e.transpose(
                            pTr[b][:, u * 128:(u + 1) * 128], sH1bt[:, oc * 128:(oc + 1) * 128], sIdent[:]),
                            inc=(u == 3))
                    _w("vector", "sPE", last)
                    ev = dve(lambda e, ocb=ocb, tt=tt, b=ocb % 2: e.tensor_copy(
                        sH1T[:].rearrange("p (a u x) -> p a u x", u=4, x=128)[:, ocb * 4:(ocb + 1) * 4, tt, :],
                        pTr[b][:, 0:512].rearrange("p (u x) -> p u x", u=4)))
                    pTr_user[ocb % 2] = ev
                    h1t_done = ev
                    h1t_last_pe = last

            _w("tensor", "sDVE", max(h1t_done, prg_prev))
            _w("tensor", "sDVE", pSm2_user)
            g_pe = pr_pe = None
            for ic in range(32):
                g_pe = pe(lambda e, ic=ic: e.matmul(pSmall[0:4, :], sWrTb[:, ic * 4:ic * 4 + 4],
                                                    sH1T[:, ic * 512:(ic + 1) * 512],
                                                    start=(ic == 0), stop=(ic == 31)), inc=(ic == 31))
            for ic in range(32):
                pr_pe = pe(lambda e, ic=ic: e.matmul(pSmall2[:], sAtb[:, ic * 128:(ic + 1) * 128],
                                                     sH1T[:, ic * 512:(ic + 1) * 512],
                                                     start=(ic == 0), stop=(ic == 31)), inc=(ic == 31))
            _w("scalar", "sPE", g_pe)
            ge = act(lambda e: e.activation(sGeb[:], pSmall[0:4, :], AF.Exp), inc=True)
            _w("tensor", "sACT", ge)
            gs_pe = pe(lambda e: e.matmul(pSmall[0:4, :], sOnes[0:4, 0:4], sGeb[:], start=True, stop=True), inc=True)
            _w("scalar", "sPE", gs_pe)
            lng = act(lambda e: e.activation(sGl[:], pSmall[0:4, :], AF.Ln), inc=True)
            _w("scalar", "sACT", lng)
            r4 = act(lambda e: e.activation(sR4[:], sGl[:], AF.Exp, scale=-1.0), inc=True)
            _w("vector", "sACT", r4)
            gn = dve(lambda e: e.tensor_tensor(sGn[:], sGeb[:], sR4[:], ALU.mult), inc=True)
            _w("tensor", "sDVE", gn)
            _w("tensor", "qE8", 16)
            grep_pe = pe(lambda e: e.matmul(pSmall[:], sE8[:], sGn[:], start=True, stop=True), inc=True)
            _w("vector", "sPE", max(pr_pe, grep_pe))
            pr_ev = dve(lambda e: e.tensor_copy(sPr32[:], pSmall2[:]), inc=True)
            pSm2_user = pr_ev
            prg = dve(lambda e: e.tensor_tensor(sPrg[:], pSmall[:], sPr32[:], ALU.mult), inc=True)
            prg_prev = prg

            evac5 = {}
            for oslab in range(8):
                for ic in range(32):
                    k, kb, cv = stripe("t", ic, oslab)
                    _w("tensor", *cv)
                    if ic == 0:
                        _w("tensor", "sDVE", prg)
                    for tt in range(4):
                        if ic == 0:
                            _w("tensor", "sDVE", pAo_user[tt])
                        val = pe(lambda e, ic=ic, tt=tt, kb=kb, st=(ic == 0):
                                 e.matmul(pAo[tt][:], sH1T[:, ic * 512 + tt * 128: ic * 512 + (tt + 1) * 128],
                                          sWb[kb][:], start=st, stop=False), inc=(tt == 3))
                    stripe_user[k] = val
                lora_done = {}
                for tt in range(4):
                    lora_done[tt] = pe(lambda e, tt=tt, oslab=oslab:
                                       e.matmul(pAo[tt][:], sPrg[:, tt * 128:(tt + 1) * 128],
                                                sBm[:, 512 * oslab:512 * (oslab + 1)],
                                                start=False, stop=True), inc=True)
                stripe_user[kglob[0] - 1] = lora_done[3]
                for tt in range(4):
                    _w("vector", "sPE", lora_done[tt])
                    ev = dve(lambda e, tt=tt, oslab=oslab: e.tensor_tensor(
                        sAct[:, tt * D + oslab * 512: tt * D + (oslab + 1) * 512],
                        pAo[tt][:], sAct[:, tt * D + oslab * 512: tt * D + (oslab + 1) * 512], ALU.add), inc=True)
                    pAo_user[tt] = ev
                    evac5[(oslab, tt)] = ev

            for tt in range(4):
                ap, apb = emit_ln(tt, evac5[(7, tt)])
                _w("scalar", "sACT", ap)
                store(lambda e, sl=sl, tt=tt: e.dma_start(
                    out=Y[512 * sl + 128 * tt: 512 * sl + 128 * (tt + 1), :],
                    in_=sAct[:, tt * D:(tt + 1) * D]), "qY")
            y_store_total = n["qY"]

    wait_all_stores("scalar")

    with contextlib.ExitStack() as emit_stack:
        sems = {}
        for name in ALL_SEMS:
            sems[name] = emit_stack.enter_context(nc.semaphore(name))
        block = emit_stack.enter_context(nc.Block())

        def mk(e):
            def emitfn(eng):
                for op in OPS[e]:
                    if op[0] == "w":
                        eng.wait_ge(sems[op[1]], op[2])
                    else:
                        ins = op[1](eng)
                        if op[2] is not None:
                            ins.then_inc(sems[op[2]], op[3])
            return emitfn

        block.sync(mk("sync"))
        block.tensor(mk("tensor"))
        block.vector(mk("vector"))
        block.scalar(mk("scalar"))
        block.gpsimd(mk("gpsimd"))

    es.close()
    return nc


def _prep_inputs(inputs):
    x = np.asarray(inputs["L0_attn_hidden_states"])[0]
    wq = np.asarray(inputs["L0_attn_wq"])
    wk = np.asarray(inputs["L0_attn_wk"])
    wv = np.asarray(inputs["L0_attn_wv"])
    wo = np.asarray(inputs["L0_attn_wo"])
    W = np.asarray(inputs["L0_ffn_W"])
    b = np.asarray(inputs["L0_ffn_b"])
    A = np.asarray(inputs["L0_ffn_A"])
    Bm = np.asarray(inputs["L0_ffn_Bm"])
    Wr = np.asarray(inputs["L0_ffn_Wr"])

    xT = np.ascontiguousarray(x.T)
    wt = np.ascontiguousarray(W.T)
    brep = np.ascontiguousarray(np.tile(b[None, :].astype(np.float32), (128, 1)))

    Apad = np.zeros((128, D), np.float32)
    for e in range(4):
        Apad[32 * e:32 * e + 8] = A[e]
    atm = np.ascontiguousarray(Apad.T.reshape(32, 128, 128).transpose(1, 0, 2).reshape(128, 32 * 128))
    bmt = np.zeros((128, D), np.float32)
    for e in range(4):
        bmt[32 * e:32 * e + 8] = Bm[e].T
    wrt = np.ascontiguousarray(Wr.T.reshape(32, 128, 4).transpose(1, 0, 2).reshape(128, 128))

    inv = 1.0 / 10000.0 ** (np.arange(0, HD, 2, dtype=np.float64) / HD)
    pos = np.arange(S, dtype=np.float64)
    fr = pos[:, None] * inv[None, :]
    emb = np.concatenate([fr, fr], axis=1)
    cosT = np.cos(emb).T.astype(np.float32)
    sinT = np.sin(emb).T.astype(np.float32)
    cs2 = np.empty((128, 2 * S), np.float32)
    for ts in range(NSLAB):
        cs2[:, 1024 * ts:1024 * ts + 512] = cosT[:, 512 * ts:512 * (ts + 1)]
        cs2[:, 1024 * ts + 512:1024 * (ts + 1)] = sinT[:, 512 * ts:512 * (ts + 1)]

    msk = np.zeros((128, 4 * QP), BF)
    ff = np.arange(QP)
    pp = np.arange(128)
    for dv in range(4):
        msk[:, dv * QP:(dv + 1) * QP] = (ff[None, :] >= (128 * dv + pp[:, None])).astype(BF)

    onesb = np.ones((128, 128), BF)
    ident = np.eye(128, dtype=np.float32).astype(BF)
    e8p = np.zeros((4, 128), BF)
    for e in range(4):
        e8p[e, 32 * e:32 * e + 8] = BF(2.0)
    cv = np.array([CEXP, SCL, -1.0, -0.5, EPS, 1.0, 1.0 / D, -1.0 / D, 0.0, 2.0, 0.5, -2.0],
                  np.float32)
    consts = np.tile(np.concatenate([cv, np.zeros(4, np.float32)]), (128, 1))

    def wslice(w, c):
        ws = w[:, FPC * c:FPC * (c + 1)]
        return np.ascontiguousarray(ws.reshape(32, 128, FPC).transpose(1, 0, 2).reshape(128, 32 * FPC))

    in_maps = []
    for c in range(NCORE):
        in_maps.append({
            "xt": xT,
            "xr": np.ascontiguousarray(x[TPC * c:TPC * (c + 1), :]),
            "wq": wslice(wq, c),
            "wk": wslice(wk, c),
            "wv": wslice(wv, c),
            "wo": wo,
            "wt": wt,
            "brep": brep,
            "atm": atm,
            "bmt": bmt,
            "wrt": wrt,
            "cs2": cs2,
            "msk": msk,
            "onesb": onesb,
            "ident": ident,
            "e8p": e8p,
            "consts": consts,
        })
    return in_maps


def kernel(**inputs):
    nc = build_program()
    in_maps = _prep_inputs(inputs)
    res = run_bass_kernel_spmd(nc, in_maps, core_ids=list(range(NCORE)))
    out = np.concatenate([np.asarray(res.results[c]["y"]) for c in range(NCORE)], axis=0)
    return out.reshape(1, S, D).astype(np.float32)
